# version 01
# baseline (speedup 1.0000x reference)
import numpy as np
from contextlib import ExitStack
import concourse.bass as bass
import concourse.mybir as mybir

F32 = mybir.dt.float32
BF16 = mybir.dt.bfloat16
U8 = mybir.dt.uint8
I32 = mybir.dt.int32
AF = mybir.ActivationFunctionType
ALU = mybir.AluOpType
AX = mybir.AxisListType
MAGIC = float(np.float32(3 * 2 ** 22))

SNN_T = [0.2441943, 0.46396917, 0.68618596, 0.9059608, 1.0839784]
def _silu64(x):
    import math
    return x / (1.0 + math.exp(-x))
SNN_D = [float(np.float32(_silu64((k + 1) / 5.0) - _silu64(k / 5.0))) for k in range(5)]


class P:
    def __init__(self):
        self.nc = bass.Bass("TRN2", target_bir_lowering=False)
        self.ctx = ExitStack()
        self.sem = {}
        self.cnt = {}

    def sbuf(self, name, shape, dt=F32):
        return self.ctx.enter_context(self.nc.sbuf_tensor(name, shape, dt))

    def psum(self, name, shape, dt=F32):
        return self.ctx.enter_context(self.nc.psum_tensor(name, shape, dt))

    def semaphore(self, name):
        s = self.ctx.enter_context(self.nc.semaphore(name))
        self.sem[name] = s
        self.cnt[name] = 0
        return s

    def inc(self, instr, name, n=1):
        instr.then_inc(self.sem[name], n)
        self.cnt[name] += n
        return self.cnt[name]

    def at(self, name):
        return self.cnt[name]


def emit_quant(p, block, src_getter, out_getter, work, rows, cols, pre_wait=None):
    nblk = cols // 16
    r = rows

    @block.vector
    def _(v):
        if pre_wait is not None:
            pre_wait(v)
        src = src_getter()
        v.tensor_reduce(work["bam"][0:r, 0:nblk], src.rearrange("p (b k) -> p b k", k=16),
                        AX.X, ALU.max, apply_absolute_value=True)
        v.tensor_scalar(work["scale"][0:r, 0:nblk], work["bam"][0:r, 0:nblk],
                        float(np.float32(1 / 6.0)), 1e-12, ALU.mult, ALU.max)
        v.reciprocal(work["r2"][0:r, 0:nblk], work["scale"][0:r, 0:nblk])
        v.tensor_scalar_mul(work["hs"][0:r, 0:nblk], work["scale"][0:r, 0:nblk], 0.5)
        v.tensor_scalar_mul(work["r2"][0:r, 0:nblk], work["r2"][0:r, 0:nblk], 2.0)
        p.inc(v.tensor_tensor(
            work["v2"][0:r, 0:cols].rearrange("p (b k) -> p b k", k=16),
            src.rearrange("p (b k) -> p b k", k=16),
            work["r2"][0:r, 0:nblk].unsqueeze(2).broadcast_to([r, nblk, 16]), ALU.mult), "dv")

    dve_hand = p.at("dv")

    @block.gpsimd
    def _(g):
        g.wait_ge(p.sem["dv"], dve_hand)
        g.tensor_tensor(work["sq"][0:r, 0:cols], work["v2"][0:r, 0:cols],
                        work["v2"][0:r, 0:cols], ALU.mult)
        g.tensor_scalar(work["m2"][0:r, 0:cols], work["sq"][0:r, 0:cols],
                        100.0, 0.5, ALU.is_gt, ALU.mult)
        g.tensor_scalar(work["a2i"][0:r, 0:cols], work["v2"][0:r, 0:cols],
                        MAGIC, MAGIC, ALU.add, ALU.subtract)
        g.tensor_scalar(work["a2"][0:r, 0:cols], work["a2i"][0:r, 0:cols],
                        4.0, -4.0, ALU.min, ALU.max)
        g.tensor_scalar(work["w1"][0:r, 0:cols], work["v2"][0:r, 0:cols],
                        0.5, MAGIC, ALU.mult, ALU.add)
        g.tensor_scalar(work["w2"][0:r, 0:cols], work["w1"][0:r, 0:cols],
                        MAGIC, 2.0, ALU.subtract, ALU.mult)
        p.inc(g.tensor_scalar(work["w2c"][0:r, 0:cols], work["w2"][0:r, 0:cols],
                              8.0, -8.0, ALU.min, ALU.max), "gp")

    gp_hand = p.at("gp")

    @block.vector
    def _(v):
        v.wait_ge(p.sem["gp"], gp_hand)
        out = out_getter()
        v.tensor_single_scalar(work["m_u8"][0:r, 0:cols], work["sq"][0:r, 0:cols],
                               25.0, ALU.is_gt)
        v.select(work["fsel"][0:r, 0:cols], work["m_u8"][0:r, 0:cols],
                 work["w2c"][0:r, 0:cols], work["a2"][0:r, 0:cols])
        v.scalar_tensor_tensor(work["v2"][0:r, 0:cols], work["m2"][0:r, 0:cols], 1.0,
                               work["fsel"][0:r, 0:cols], ALU.add, ALU.mult)
        p.inc(v.tensor_tensor(
            out.rearrange("p (b k) -> p b k", k=16),
            work["v2"][0:r, 0:cols].rearrange("p (b k) -> p b k", k=16),
            work["hs"][0:r, 0:nblk].unsqueeze(2).broadcast_to([r, nblk, 16]), ALU.mult), "dv")

    return p.at("dv")


def quant_work_tiles(p, maxcols=2816, prefix="qw"):
    w = {k: p.sbuf(f"{prefix}_{k}", [128, maxcols], F32) for k in
         ["v2", "sq", "m2", "t1", "t2", "t3"]}
    w["a2i"] = w["t1"]; w["a2"] = w["t2"]; w["w1"] = w["t1"]
    w["w2"] = w["t3"]; w["w2c"] = w["t1"]; w["fsel"] = w["t3"]
    w["m_u8"] = p.sbuf(f"{prefix}_m_u8", [128, maxcols], U8)
    for k in ["bam", "scale", "hs", "r2"]:
        w[k] = p.sbuf(f"{prefix}_{k}", [128, maxcols // 16], F32)
    return w


def emit_ln(p, block, xin, xout_list, stats, x2, eps=1e-5, gw=None, gb=None):
    D = xin.shape[1]

    @block.scalar
    def _(s):
        s.wait_ge(p.sem["dv"], p.at("dv"))
        p.inc(s.activation(x2[:, 0:D], xin, AF.Square, accum_out=stats[:, 1:2]), "ac")

    @block.vector
    def _(v):
        p.inc(v.reduce_sum(stats[:, 0:1], xin, AX.X), "dv")
        v.wait_ge(p.sem["ac"], p.at("ac"))
        v.wait_ge(p.sem["dv"], p.at("dv"))
        v.tensor_scalar_mul(stats[:, 2:3], stats[:, 0:1], 1.0 / D)
        p.inc(v.tensor_scalar_mul(stats[:, 3:4], stats[:, 1:2], 1.0 / D), "dv")
        v.wait_ge(p.sem["dv"], p.at("dv"))
        v.tensor_tensor(stats[:, 4:5], stats[:, 2:3], stats[:, 2:3], ALU.mult)
        p.inc(v.tensor_single_scalar(stats[:, 5:6], stats[:, 3:4], eps, ALU.add), "dv")
        v.wait_ge(p.sem["dv"], p.at("dv"))
        p.inc(v.tensor_tensor(stats[:, 5:6], stats[:, 5:6], stats[:, 4:5], ALU.subtract), "dv")
        v.wait_ge(p.sem["dv"], p.at("dv"))
        p.inc(v.reciprocal(stats[:, 6:7], stats[:, 5:6]), "dv")

    @block.scalar
    def _(s):
        s.wait_ge(p.sem["dv"], p.at("dv"))
        p.inc(s.activation(stats[:, 7:8], stats[:, 6:7], AF.Sqrt), "ac")

    @block.vector
    def _(v):
        v.wait_ge(p.sem["ac"], p.at("ac"))
        for ap in xout_list:
            ins = v.tensor_scalar(ap, xin, stats[:, 2:3], stats[:, 7:8],
                                  ALU.subtract, ALU.mult)
        p.inc(ins, "dv")

    return p.at("dv")


def build_launch_A(_PH1=16, _PH2=True):
    p = P()
    nc = p.nc
    x = nc.dram_tensor("x_own", [2048, 1024], F32, kind="ExternalInput").ap()
    wqT = nc.dram_tensor("wqT", [1024, 1024], F32, kind="ExternalInput").ap()
    wkT = nc.dram_tensor("wkT", [1024, 1024], F32, kind="ExternalInput").ap()
    wvT = nc.dram_tensor("wvT", [1024, 1024], F32, kind="ExternalInput").ap()
    wg_sh = nc.dram_tensor("wg_sh", [352, 1024], F32, kind="ExternalInput").ap()
    wu_sh = nc.dram_tensor("wu_sh", [352, 1024], F32, kind="ExternalInput").ap()
    wdn_sh = nc.dram_tensor("wdn_sh", [128, 2816], F32, kind="ExternalInput").ap()
    q_fm = nc.dram_tensor("q_fm", [1024, 2048], BF16, kind="ExternalOutput").ap()
    k_fm = nc.dram_tensor("k_fm", [1024, 2048], BF16, kind="ExternalOutput").ap()
    v_tm = nc.dram_tensor("v_tm", [2048, 1024], BF16, kind="ExternalOutput").ap()
    xq_tm = nc.dram_tensor("xq_tm", [2048, 1024], BF16, kind="ExternalOutput").ap()
    xn_tm = nc.dram_tensor("xn_tm", [2048, 1024], BF16, kind="ExternalOutput").ap()
    wgq_o = nc.dram_tensor("wgq_sh", [352, 1024], BF16, kind="ExternalOutput").ap()
    dbg_wq = nc.dram_tensor("dbg_wq", [128, 8192], BF16, kind="ExternalOutput").ap()
    dbg_xnT = nc.dram_tensor("dbg_xnT", [128, 16384], BF16, kind="ExternalOutput").ap()
    wuq_o = nc.dram_tensor("wuq_sh", [352, 1024], BF16, kind="ExternalOutput").ap()
    wdnq_o = nc.dram_tensor("wdnq_sh", [128, 2816], BF16, kind="ExternalOutput").ap()

    xt = [p.sbuf(f"xt{i}", [128, 1024], F32) for i in range(2)]
    xnb = p.sbuf("xnb", [128, 1024], BF16)
    xnf = p.sbuf("xnf", [128, 1024], F32)
    x2 = p.sbuf("x2", [128, 1024], F32)
    stats = p.sbuf("stats", [128, 8], F32)
    qw = quant_work_tiles(p)
    qout = p.sbuf("qout", [128, 2816], BF16)
    wsrc = p.sbuf("wsrc", [128, 2816], F32)
    wq_b = p.sbuf("wq_b", [128, 8 * 1024], BF16)
    wk_b = p.sbuf("wk_b", [128, 8 * 1024], BF16)
    wv_b = p.sbuf("wv_b", [128, 8 * 1024], BF16)
    wtmp = p.sbuf("wtmp", [128, 1024], F32)
    xnT = p.sbuf("xnT", [128, 8 * 2048], BF16)
    evac = p.sbuf("evac", [128, 512], BF16)
    ps = [p.psum(f"ps{i}", [128, 512], F32) for i in range(4)]

    for s in ["ld", "dv", "gp", "ac", "te", "st", "lx"]:
        p.semaphore(s)

    with p.ctx, nc.Block() as block:
        for i in range(_PH1):
            buf = xt[i % 2]

            @block.sync
            def _(sync, i=i, buf=buf):
                sync.wait_ge(p.sem["dv"], p.at("dv"))
                sync.dma_start(out=buf[:], in_=x[i * 128:(i + 1) * 128, :]).then_inc(p.sem["ld"], 16)
                p.cnt["ld"] += 16

            ld_t = p.at("ld")

            @block.vector
            def _(v, ld_t=ld_t):
                v.wait_ge(p.sem["ld"], ld_t)
                v.wait_ge(p.sem["st"], p.at("st"))

            @block.scalar
            def _(s, ld_t=ld_t):
                s.wait_ge(p.sem["ld"], ld_t)

            emit_ln(p, block, buf[:], [xnb[:], xnf[:]], stats, x2)

            @block.sync
            def _(sync, i=i):
                sync.wait_ge(p.sem["dv"], p.at("dv"))
                sync.dma_start(out=xn_tm[i * 128:(i + 1) * 128, :], in_=xnb[:]).then_inc(p.sem["st"], 16)
                p.cnt["st"] += 16

            emit_quant(p, block, lambda: xnf[:], lambda: qout[:, 0:1024], qw, 128, 1024)

            @block.sync
            def _(sync, i=i):
                sync.wait_ge(p.sem["dv"], p.at("dv"))
                sync.dma_start(out=xq_tm[i * 128:(i + 1) * 128, :], in_=qout[:, 0:1024]).then_inc(p.sem["st"], 16)
                p.cnt["st"] += 16

        for (src, dst, rows, cols) in ([(wg_sh, wgq_o, 352, 1024), (wu_sh, wuq_o, 352, 1024),
                                       (wdn_sh, wdnq_o, 128, 2816)] if _PH2 else []):
            r0 = 0
            while r0 < rows:
                r = min(128, rows - r0)

                @block.sync
                def _(sync, src=src, r0=r0, r=r, cols=cols):
                    sync.wait_ge(p.sem["dv"], p.at("dv"))
                    sync.dma_start(out=wsrc[0:r, 0:cols], in_=src[r0:r0 + r, :]).then_inc(p.sem["ld"], 16)
                    p.cnt["ld"] += 16

                ld_t = p.at("ld")

                @block.vector
                def _(v, ld_t=ld_t):
                    v.wait_ge(p.sem["ld"], ld_t)
                    v.wait_ge(p.sem["st"], p.at("st"))

                @block.gpsimd
                def _(g, ld_t=ld_t):
                    g.wait_ge(p.sem["ld"], ld_t)

                emit_quant(p, block, lambda r=r, cols=cols: wsrc[0:r, 0:cols],
                           lambda r=r, cols=cols: qout[0:r, 0:cols], qw, r, cols)

                @block.sync
                def _(sync, dst=dst, r0=r0, r=r, cols=cols):
                    sync.wait_ge(p.sem["dv"], p.at("dv"))
                    sync.dma_start(out=dst[r0:r0 + r, :], in_=qout[0:r, 0:cols]).then_inc(p.sem["st"], 16)
                    p.cnt["st"] += 16

                r0 += r

        for (wsrc_d, wdst) in [(wqT, wq_b), (wkT, wk_b), (wvT, wv_b)]:
            for c in range(8):
                @block.sync
                def _(sync, wsrc_d=wsrc_d, c=c):
                    sync.wait_ge(p.sem["dv"], p.at("dv"))
                    sync.dma_start(out=wtmp[:], in_=wsrc_d[c * 128:(c + 1) * 128, :]).then_inc(p.sem["ld"], 16)
                    p.cnt["ld"] += 16

                ld_t = p.at("ld")

                @block.vector
                def _(v, wdst=wdst, c=c, ld_t=ld_t):
                    v.wait_ge(p.sem["ld"], ld_t)
                    p.inc(v.tensor_copy(wdst[:, c * 1024:(c + 1) * 1024], wtmp[:]), "dv")

        @block.scalar
        def _(s):
            s.wait_ge(p.sem["st"], p.at("st"))
            for d in range(8):
                for tb in range(4):
                    s.dma_start_transpose(
                        xnT[:, d * 2048 + tb * 512:d * 2048 + (tb + 1) * 512],
                        xn_tm[tb * 512:(tb + 1) * 512, d * 128:(d + 1) * 128],
                    ).then_inc(p.sem["lx"], 16)
                    p.cnt["lx"] += 16

        xnT_ld = p.at("lx")

        for (wb, outd, scale) in [(wq_b, q_fm, 0.125), (wk_b, k_fm, None)]:
            for e in range(8):
                for tb in range(4):
                    @block.tensor
                    def _(t, wb=wb, e=e, tb=tb):
                        t.wait_ge(p.sem["lx"], xnT_ld)
                        t.wait_ge(p.sem["dv"], p.at("dv"))
                        bank = ps[(e * 4 + tb) % 2]
                        ins = None
                        for d in range(8):
                            ins = t.matmul(bank[:],
                                           wb[:, d * 1024 + e * 128:d * 1024 + (e + 1) * 128],
                                           xnT[:, d * 2048 + tb * 512:d * 2048 + (tb + 1) * 512],
                                           start=(d == 0), stop=(d == 7))
                        p.inc(ins, "te")

                    @block.vector
                    def _(v, e=e, tb=tb, scale=scale):
                        v.wait_ge(p.sem["te"], p.at("te"))
                        v.wait_ge(p.sem["st"], p.at("st"))
                        bank = ps[(e * 4 + tb) % 2]
                        if scale is not None:
                            ins = v.tensor_scalar_mul(evac[:], bank[:], scale)
                        else:
                            ins = v.tensor_copy(evac[:], bank[:])
                        p.inc(ins, "dv")

                    @block.sync
                    def _(sync, e=e, tb=tb, outd=outd):
                        sync.wait_ge(p.sem["dv"], p.at("dv"))
                        sync.dma_start(out=outd[e * 128:(e + 1) * 128, tb * 512:(tb + 1) * 512],
                                       in_=evac[:]).then_inc(p.sem["st"], 16)
                        p.cnt["st"] += 16

        for tc in range(16):
            for eh in range(2):
                @block.tensor
                def _(t, tc=tc, eh=eh):
                    t.wait_ge(p.sem["dv"], p.at("dv"))
                    bank = ps[2 + (tc * 2 + eh) % 2]
                    ins = None
                    for d in range(8):
                        ins = t.matmul(bank[:],
                                       xnT[:, d * 2048 + tc * 128:d * 2048 + (tc + 1) * 128],
                                       wv_b[:, d * 1024 + eh * 512:d * 1024 + (eh + 1) * 512],
                                       start=(d == 0), stop=(d == 7))
                    p.inc(ins, "te")

                @block.vector
                def _(v, tc=tc, eh=eh):
                    v.wait_ge(p.sem["te"], p.at("te"))
                    v.wait_ge(p.sem["st"], p.at("st"))
                    bank = ps[2 + (tc * 2 + eh) % 2]
                    p.inc(v.tensor_copy(evac[:], bank[:]), "dv")

                @block.sync
                def _(sync, tc=tc, eh=eh):
                    sync.wait_ge(p.sem["dv"], p.at("dv"))
                    sync.dma_start(out=v_tm[tc * 128:(tc + 1) * 128, eh * 512:(eh + 1) * 512],
                                   in_=evac[:]).then_inc(p.sem["st"], 16)
                    p.cnt["st"] += 16

        @block.sync
        def _(sync):
            sync.dma_start(out=dbg_wq[:], in_=wq_b[:]).then_inc(p.sem["st"], 16)
            p.cnt["st"] += 16
            sync.dma_start(out=dbg_xnT[:], in_=xnT[:]).then_inc(p.sem["st"], 16)
            p.cnt["st"] += 16
            sync.wait_ge(p.sem["st"], p.at("st"))

    return p.nc


def build_launch_B():
    p = P()
    nc = p.nc
    q_fm = nc.dram_tensor("q_fm", [1024, 2048], BF16, kind="ExternalInput").ap()
    k_fm = nc.dram_tensor("k_fm", [1024, 8192], BF16, kind="ExternalInput").ap()
    v_tm = nc.dram_tensor("v_tm", [8192, 1024], BF16, kind="ExternalInput").ap()
    o_tm = nc.dram_tensor("o_tm", [2048, 1024], BF16, kind="ExternalOutput").ap()

    kres = p.sbuf("kres", [128, 8 * 8192], BF16)
    vt = [p.sbuf(f"vt{i}", [128, 1024], BF16) for i in range(4)]
    qblk = [p.sbuf(f"qblk{i}", [128, 8 * 256], BF16) for i in range(2)]
    ptile = [p.sbuf(f"pt{i}", [128, 256], BF16) for i in range(2)]
    ones_bf = p.sbuf("ones_bf", [128, 1], BF16)
    ones1f = p.sbuf("ones1f", [1, 1], F32)
    denrow = p.sbuf("denrow", [1, 256], F32)
    rden = p.sbuf("rden", [128, 2], F32)
    oevac = p.sbuf("oevac", [128, 512], BF16)
    pv_ps = [p.psum(f"pv{i}", [128, 512], F32) for i in range(4)]
    s_ps = [p.psum(f"s{i}", [128, 256], F32) for i in range(2)]
    den_ps = p.psum("den", [1, 256], F32)
    rd_ps = p.psum("rd", [128, 2], F32)

    for s in ["ls", "dv", "ac", "te", "st"]:
        p.semaphore(s)

    with p.ctx, nc.Block() as block:
        @block.sync
        def _(sync):
            for c in range(8):
                sync.dma_start(out=kres[:, c * 8192:(c + 1) * 8192],
                               in_=k_fm[c * 128:(c + 1) * 128, :]).then_inc(p.sem["ls"], 16)
                p.cnt["ls"] += 16

        @block.vector
        def _(v):
            v.memset(ones_bf[:], 1.0)
            p.inc(v.memset(ones1f[:], 1.0), "dv")

        kld = p.at("ls")
        NS = 64
        dv_evac = p.at("dv")

        for qb in range(8):
            qb_buf = qblk[qb % 2]

            @block.sync
            def _(sync, qb=qb, qb_buf=qb_buf):
                sync.wait_ge(p.sem["te"], p.at("te"))
                for d in range(8):
                    sync.dma_start(out=qb_buf[:, d * 256:(d + 1) * 256],
                                   in_=q_fm[d * 128:(d + 1) * 128, qb * 256:(qb + 1) * 256]
                                   ).then_inc(p.sem["ls"], 16)
                    p.cnt["ls"] += 16

            qld = p.at("ls")
            te_exp = {}
            ac_exp = {}

            for sc in range(NS + 1):
                if sc < NS:
                    @block.sync
                    def _(sync, qb=qb, sc=sc):
                        if sc >= 4:
                            sync.wait_ge(p.sem["te"], te_exp[("pv", sc - 4)])
                        sync.dma_start(out=vt[sc % 4][:],
                                       in_=v_tm[sc * 128:(sc + 1) * 128, :]).then_inc(p.sem["ls"], 16)
                        p.cnt["ls"] += 16
                    vld = p.at("ls")

                    @block.tensor
                    def _(t, qb=qb, sc=sc, qld=qld):
                        t.wait_ge(p.sem["ls"], qld)
                        t.wait_ge(p.sem["ls"], kld)
                        if sc >= 2:
                            t.wait_ge(p.sem["ac"], ac_exp[sc - 2])
                        ins = None
                        for c in range(8):
                            ins = t.matmul(s_ps[sc % 2][:],
                                           kres[:, c * 8192 + sc * 128:c * 8192 + (sc + 1) * 128],
                                           qblk[qb % 2][:, c * 256:(c + 1) * 256],
                                           start=(c == 0), stop=(c == 7))
                        te_exp[("s", sc)] = p.inc(ins, "te")

                sp = sc - 1
                if sp >= 0:
                    @block.tensor
                    def _(t, qb=qb, sp=sp):
                        t.wait_ge(p.sem["ac"], ac_exp[sp])
                        t.wait_ge(p.sem["ls"], vl_exp[sp])
                        if sp == 0:
                            t.wait_ge(p.sem["dv"], dv_evac)
                        for qs in range(2):
                            for eh in range(2):
                                t.matmul(pv_ps[qs * 2 + eh][:],
                                         ptile[sp % 2][:, qs * 128:(qs + 1) * 128],
                                         vt[sp % 4][:, eh * 512:(eh + 1) * 512],
                                         start=(sp == 0), stop=(sp == NS - 1))
                        ins = t.matmul(den_ps[:], ones_bf[:], ptile[sp % 2][:],
                                       start=(sp == 0), stop=(sp == NS - 1))
                        te_exp[("pv", sp)] = p.inc(ins, "te")

                if sc < NS:
                    @block.scalar
                    def _(s, qb=qb, sc=sc):
                        s.wait_ge(p.sem["te"], te_exp[("s", sc)])
                        if sc >= 2:
                            s.wait_ge(p.sem["te"], te_exp[("pv", sc - 2)])
                        ac_exp[sc] = p.inc(s.activation(ptile[sc % 2][:], s_ps[sc % 2][:],
                                                        AF.Exp), "ac")
                    if sc == 0:
                        pass

                if sc == 0:
                    vl_exp = {}
                if sc < NS:
                    vl_exp[sc] = vld

            @block.vector
            def _(v, qb=qb):
                v.wait_ge(p.sem["te"], te_exp[("pv", NS - 1)])
                v.wait_ge(p.sem["st"], p.at("st"))
                p.inc(v.tensor_copy(denrow[:], den_ps[:]), "dv")

            @block.tensor
            def _(t, qb=qb):
                t.wait_ge(p.sem["dv"], p.at("dv"))
                for qs in range(2):
                    ins = t.transpose(rd_ps[:, qs:qs + 1], denrow[:, qs * 128:(qs + 1) * 128],
                                      ones1f[:])
                p.inc(ins, "te")

            @block.vector
            def _(v, qb=qb):
                v.wait_ge(p.sem["te"], p.at("te"))
                p.inc(v.reciprocal(rden[:], rd_ps[:]), "dv")

            for qs in range(2):
                for eh in range(2):
                    @block.vector
                    def _(v, qb=qb, qs=qs, eh=eh):
                        v.wait_ge(p.sem["dv"], p.at("dv"))
                        v.wait_ge(p.sem["st"], p.at("st"))
                        p.inc(v.tensor_scalar_mul(oevac[:], pv_ps[qs * 2 + eh][:],
                                                  rden[:, qs:qs + 1]), "dv")

                    @block.sync
                    def _(sync, qb=qb, qs=qs, eh=eh):
                        sync.wait_ge(p.sem["dv"], p.at("dv"))
                        sync.dma_start(out=o_tm[qb * 256 + qs * 128:qb * 256 + (qs + 1) * 128,
                                                eh * 512:(eh + 1) * 512],
                                       in_=oevac[:]).then_inc(p.sem["st"], 16)
                        p.cnt["st"] += 16

            dv_evac = p.at("dv")

        @block.sync
        def _(sync):
            sync.wait_ge(p.sem["st"], p.at("st"))

    return p.nc


def build_launch_C():
    p = P()
    nc = p.nc
    y_fm = nc.dram_tensor("y_fm", [1024, 2048], BF16, kind="ExternalInput").ap()
    woT = nc.dram_tensor("woT", [1024, 1024], F32, kind="ExternalInput").ap()
    x = nc.dram_tensor("x_own", [2048, 1024], F32, kind="ExternalInput").ap()
    xq_fm = nc.dram_tensor("xq_fm", [1024, 2048], BF16, kind="ExternalInput").ap()
    wgq_fm = nc.dram_tensor("wgq_fm", [1024, 2816], BF16, kind="ExternalInput").ap()
    wuq_fm = nc.dram_tensor("wuq_fm", [1024, 2816], BF16, kind="ExternalInput").ap()
    wdnqT = nc.dram_tensor("wdnqT", [2816, 1024], BF16, kind="ExternalInput").ap()
    x1_o = nc.dram_tensor("x1", [2048, 1024], F32, kind="ExternalOutput").ap()
    s0_d = nc.dram_tensor("s0_scr", [2048, 1024], BF16, kind="Internal").ap()
    gate_d = nc.dram_tensor("gate_scr", [2048, 2816], BF16, kind="Internal").ap()
    hq_d = nc.dram_tensor("hq_scr", [2048, 2816], BF16, kind="Internal").ap()

    arena = p.sbuf("arena", [128, 24576], BF16)
    wtmp = p.sbuf("wtmp", [128, 1024], F32)
    qw = quant_work_tiles(p)
    htile = p.sbuf("htile", [128, 2816], F32)
    hqt_tm = p.sbuf("hqt_tm", [128, 2816], BF16)
    gtile = p.sbuf("gtile", [128, 2816], BF16)
    xqt = [p.sbuf(f"xqt{i}", [128, 1024], BF16) for i in range(2)]
    hqt_fm = p.sbuf("hqt_fm", [128, 2816], BF16)
    evac = p.sbuf("evacC", [128, 512], BF16)
    xt = p.sbuf("xtC", [128, 1024], F32)
    s0t = p.sbuf("s0t", [128, 1024], BF16)
    g0t = p.sbuf("g0t", [128, 1024], F32)
    t1t = p.sbuf("t1t", [128, 1024], F32)
    ps = [p.psum(f"psC{i}", [128, 512], F32) for i in range(4)]

    for s in ["ls", "lx", "dv", "gp", "ac", "te", "st"]:
        p.semaphore(s)

    NSL = [(i * 512, min(512, 2816 - i * 512)) for i in range(6)]

    with p.ctx, nc.Block() as block:
        ybuf = arena[:, 0:16384]
        wob = arena[:, 16384:24576]

        @block.sync
        def _(sync):
            for j in range(8):
                sync.dma_start(out=ybuf[:, j * 2048:(j + 1) * 2048],
                               in_=y_fm[j * 128:(j + 1) * 128, :]).then_inc(p.sem["ls"], 16)
                p.cnt["ls"] += 16

        for j in range(8):
            @block.sync
            def _(sync, j=j):
                sync.wait_ge(p.sem["dv"], p.at("dv"))
                sync.dma_start(out=wtmp[:], in_=woT[j * 128:(j + 1) * 128, :]).then_inc(p.sem["ls"], 16)
                p.cnt["ls"] += 16
            ld_t = p.at("ls")

            @block.vector
            def _(v, j=j, ld_t=ld_t):
                v.wait_ge(p.sem["ls"], ld_t)
                p.inc(v.tensor_copy(wob[:, j * 1024:(j + 1) * 1024], wtmp[:]), "dv")

        a_ready = p.at("dv")
        a_lsready = p.at("ls")

        for tc in range(16):
            for dh in range(2):
                @block.tensor
                def _(t, tc=tc, dh=dh):
                    t.wait_ge(p.sem["dv"], p.at("dv"))
                    t.wait_ge(p.sem["ls"], a_lsready)
                    bank = ps[(tc * 2 + dh) % 2]
                    ins = None
                    for j in range(8):
                        ins = t.matmul(bank[:],
                                       ybuf[:, j * 2048 + tc * 128:j * 2048 + (tc + 1) * 128],
                                       wob[:, j * 1024 + dh * 512:j * 1024 + (dh + 1) * 512],
                                       start=(j == 0), stop=(j == 7))
                    p.inc(ins, "te")

                @block.vector
                def _(v, tc=tc, dh=dh):
                    v.wait_ge(p.sem["te"], p.at("te"))
                    v.wait_ge(p.sem["st"], p.at("st"))
                    bank = ps[(tc * 2 + dh) % 2]
                    p.inc(v.tensor_copy(evac[:], bank[:]), "dv")

                @block.sync
                def _(sync, tc=tc, dh=dh):
                    sync.wait_ge(p.sem["dv"], p.at("dv"))
                    sync.dma_start(out=s0_d[tc * 128:(tc + 1) * 128, dh * 512:(dh + 1) * 512],
                                   in_=evac[:]).then_inc(p.sem["st"], 16)
                    p.cnt["st"] += 16

        def ffn_pass(wsrc, out_dram, combine):
            wb = arena[:, 0:22528]

            @block.sync
            def _(sync):
                sync.wait_ge(p.sem["te"], p.at("te"))
                for kc in range(8):
                    sync.dma_start(out=wb[:, kc * 2816:(kc + 1) * 2816],
                                   in_=wsrc[kc * 128:(kc + 1) * 128, :]).then_inc(p.sem["ls"], 16)
                    p.cnt["ls"] += 16
            wld = p.at("ls")

            for tc in range(16):
                @block.sync
                def _(sync, tc=tc):
                    sync.wait_ge(p.sem["te"], p.at("te"))
                    sync.dma_start(out=xqt[tc % 2][:].rearrange("p (c t) -> p c t", t=128),
                                   in_=xq_fm.rearrange("(c p) t -> p c t", p=128)[:, :, tc * 128:(tc + 1) * 128]
                                   ).then_inc(p.sem["ls"], 16)
                    p.cnt["ls"] += 16
                xld = p.at("ls")
                if combine == "up":
                    @block.sync
                    def _(sync, tc=tc):
                        sync.wait_ge(p.sem["gp"], p.at("gp"))
                        sync.dma_start(out=gtile[:], in_=gate_d[tc * 128:(tc + 1) * 128, :]
                                       ).then_inc(p.sem["ls"], 16)
                        p.cnt["ls"] += 16
                gld = p.at("ls")

                for (n0, nn) in NSL:
                    @block.tensor
                    def _(t, tc=tc, n0=n0, nn=nn, xld=xld, wld=wld):
                        t.wait_ge(p.sem["ls"], xld)
                        t.wait_ge(p.sem["ls"], wld)
                        if combine == "silu":
                            t.wait_ge(p.sem["ac"], p.at("ac"))
                        else:
                            t.wait_ge(p.sem["dv"], p.at("dv"))
                        bank = ps[(n0 // 512) % 2 + (2 if combine == "up" else 0)]
                        ins = None
                        for kc in range(8):
                            ins = t.matmul(bank[0:128, 0:nn],
                                           xqt[tc % 2][:, kc * 128:(kc + 1) * 128],
                                           arena[:, kc * 2816 + n0:kc * 2816 + n0 + nn],
                                           start=(kc == 0), stop=(kc == 7))
                        p.inc(ins, "te")

                    if combine == "silu":
                        @block.scalar
                        def _(s, tc=tc, n0=n0, nn=nn):
                            s.wait_ge(p.sem["te"], p.at("te"))
                            s.wait_ge(p.sem["st"], p.at("st"))
                            bank = ps[(n0 // 512) % 2]
                            p.inc(s.activation(evac[:, 0:nn], bank[0:128, 0:nn], AF.Silu), "ac")

                        @block.sync
                        def _(sync, tc=tc, n0=n0, nn=nn):
                            sync.wait_ge(p.sem["ac"], p.at("ac"))
                            sync.dma_start(out=out_dram[tc * 128:(tc + 1) * 128, n0:n0 + nn],
                                           in_=evac[:, 0:nn]).then_inc(p.sem["st"], 16)
                            p.cnt["st"] += 16
                    else:
                        @block.vector
                        def _(v, tc=tc, n0=n0, nn=nn, gld=gld):
                            v.wait_ge(p.sem["te"], p.at("te"))
                            v.wait_ge(p.sem["ls"], gld)
                            bank = ps[(n0 // 512) % 2 + 2]
                            p.inc(v.tensor_tensor(htile[:, n0:n0 + nn], bank[0:128, 0:nn],
                                                  gtile[:, n0:n0 + nn], ALU.mult), "dv")

                if combine == "up":
                    emit_quant(p, block, lambda: htile[:], lambda: hqt_tm[:], qw, 128, 2816)

                    @block.sync
                    def _(sync, tc=tc):
                        sync.wait_ge(p.sem["dv"], p.at("dv"))
                        sync.dma_start(out=hq_d[tc * 128:(tc + 1) * 128, :],
                                       in_=hqt_tm[:]).then_inc(p.sem["st"], 16)
                        p.cnt["st"] += 16

                    @block.gpsimd
                    def _(g, tc=tc):
                        g.wait_ge(p.sem["dv"], p.at("dv"))
                        p.inc(g.engine_nop(), "gp")

        ffn_pass(wgq_fm, gate_d, "silu")
        ffn_pass(wuq_fm, None, "up")

        wdn = arena[:, 0:22528]

        @block.sync
        def _(sync):
            sync.wait_ge(p.sem["te"], p.at("te"))
            for kc in range(22):
                sync.dma_start(out=wdn[:, kc * 1024:(kc + 1) * 1024],
                               in_=wdnqT[kc * 128:(kc + 1) * 128, :]).then_inc(p.sem["ls"], 16)
                p.cnt["ls"] += 16
        wld2 = p.at("ls")

        for tc in range(16):
            @block.scalar
            def _(s, tc=tc):
                s.wait_ge(p.sem["st"], p.at("st"))
                s.wait_ge(p.sem["te"], p.at("te"))
                for kc in range(22):
                    s.dma_start_transpose(hqt_fm[:, kc * 128:(kc + 1) * 128],
                                          hq_d[tc * 128:(tc + 1) * 128, kc * 128:(kc + 1) * 128]
                                          ).then_inc(p.sem["lx"], 16)
                    p.cnt["lx"] += 16

            @block.sync
            def _(sync, tc=tc):
                sync.wait_ge(p.sem["dv"], p.at("dv"))
                sync.dma_start(out=xt[:], in_=x[tc * 128:(tc + 1) * 128, :]).then_inc(p.sem["ls"], 16)
                p.cnt["ls"] += 16
                sync.dma_start(out=s0t[:], in_=s0_d[tc * 128:(tc + 1) * 128, :]).then_inc(p.sem["ls"], 16)
                p.cnt["ls"] += 16
            rld = p.at("ls")
            txld = p.at("lx")

            for dh in range(2):
                @block.tensor
                def _(t, tc=tc, dh=dh, txld=txld, wld2=wld2):
                    t.wait_ge(p.sem["lx"], txld)
                    t.wait_ge(p.sem["ls"], wld2)
                    t.wait_ge(p.sem["ac"], p.at("ac"))
                    bank = ps[dh]
                    ins = None
                    for kc in range(22):
                        ins = t.matmul(bank[:],
                                       hqt_fm[:, kc * 128:(kc + 1) * 128],
                                       wdn[:, kc * 1024 + dh * 512:kc * 1024 + (dh + 1) * 512],
                                       start=(kc == 0), stop=(kc == 21))
                    p.inc(ins, "te")

                @block.scalar
                def _(s, tc=tc, dh=dh):
                    s.wait_ge(p.sem["te"], p.at("te"))
                    s.wait_ge(p.sem["dv"], p.at("dv"))
                    bank = ps[dh]
                    p.inc(s.activation(g0t[:, dh * 512:(dh + 1) * 512], bank[:], AF.Silu), "ac")

            @block.vector
            def _(v, tc=tc, rld=rld):
                v.wait_ge(p.sem["ac"], p.at("ac"))
                v.wait_ge(p.sem["ls"], rld)
                v.tensor_tensor(t1t[:], s0t[:], g0t[:], ALU.mult)
                p.inc(v.tensor_tensor(t1t[:], t1t[:], xt[:], ALU.add), "dv")

            @block.sync
            def _(sync, tc=tc):
                sync.wait_ge(p.sem["dv"], p.at("dv"))
                sync.dma_start(out=x1_o[tc * 128:(tc + 1) * 128, :], in_=t1t[:]).then_inc(p.sem["st"], 16)
                p.cnt["st"] += 16

        @block.sync
        def _(sync):
            sync.wait_ge(p.sem["st"], p.at("st"))

    return p.nc


def build_launch_D():
    p = P()
    nc = p.nc
    x1 = nc.dram_tensor("x1", [2048, 1024], F32, kind="ExternalInput").ap()
    lnw = nc.dram_tensor("lnw", [1, 1024], F32, kind="ExternalInput").ap()
    lnb = nc.dram_tensor("lnb", [1, 1024], F32, kind="ExternalInput").ap()
    WfcT = nc.dram_tensor("WfcT", [1024, 4096], F32, kind="ExternalInput").ap()
    WprojT = nc.dram_tensor("WprojT", [4096, 1024], F32, kind="ExternalInput").ap()
    xn1_o = nc.dram_tensor("xn1", [2048, 1024], F32, kind="ExternalOutput").ap()
    nrm_o = nc.dram_tensor("normed_fm", [1024, 2048], BF16, kind="ExternalOutput").ap()
    g1_o = nc.dram_tensor("g1", [2048, 1024], F32, kind="ExternalOutput").ap()
    nrm_tm_d = nc.dram_tensor("nrm_tm_scr", [2048, 1024], BF16, kind="Internal").ap()
    xn1b_d = nc.dram_tensor("xn1b_scr", [2048, 1024], BF16, kind="Internal").ap()

    xt = [p.sbuf(f"xtD{i}", [128, 1024], F32) for i in range(2)]
    xn1t = p.sbuf("xn1t", [128, 1024], F32)
    xn1b = p.sbuf("xn1b", [128, 1024], BF16)
    nrmt = p.sbuf("nrmt", [128, 1024], BF16)
    x2 = p.sbuf("x2D", [128, 1024], F32)
    stats = p.sbuf("statsD", [128, 8], F32)
    lncols = p.sbuf("lncols", [128, 16], F32)
    wfc = p.sbuf("wfc", [128, 8 * 4096], BF16)
    wpr = p.sbuf("wpr", [128, 32 * 1024], BF16)
    wtmp = p.sbuf("wtmpD", [128, 2048], F32)
    xnf_t = p.sbuf("xnf_t", [128, 8 * 512], BF16)
    h2 = p.sbuf("h2D", [128, 32 * 512], BF16)
    hev = p.sbuf("hevD", [128, 512], F32)
    yse = p.sbuf("yseD", [128, 512], F32)
    sacc = p.sbuf("saccD", [128, 512], F32)
    smsk = p.sbuf("smskD", [128, 512], F32)
    tpose = [p.sbuf(f"tpD{i}", [128, 512], BF16) for i in range(2)]
    ps = [p.psum(f"psD{i}", [128, 512], F32) for i in range(4)]

    for s in ["ls", "lx", "dv", "gp", "ac", "te", "st"]:
        p.semaphore(s)

    with p.ctx, nc.Block() as block:
        @block.sync
        def _(sync):
            sync.dma_start(out=lncols[:, 0:8], in_=lnw.rearrange("o (c p) -> (o p) c", p=128),
                           allow_slow_non_contiguous=True).then_inc(p.sem["ls"], 16)
            p.cnt["ls"] += 16
            sync.dma_start(out=lncols[:, 8:16], in_=lnb.rearrange("o (c p) -> (o p) c", p=128),
                           allow_slow_non_contiguous=True).then_inc(p.sem["ls"], 16)
            p.cnt["ls"] += 16
        lncols_ld = p.at("ls")

        for (wsrc_d, wdst, nch, w) in [(WfcT, wfc, 8, 4096), (WprojT, wpr, 32, 1024)]:
            for c in range(nch):
                for half in range(0, w, 2048):
                    hw = min(2048, w - half)

                    @block.sync
                    def _(sync, wsrc_d=wsrc_d, c=c, w=w, half=half, hw=hw):
                        sync.wait_ge(p.sem["dv"], p.at("dv"))
                        sync.dma_start(out=wtmp[:, 0:hw],
                                       in_=wsrc_d[c * 128:(c + 1) * 128, half:half + hw]).then_inc(p.sem["ls"], 16)
                        p.cnt["ls"] += 16
                    ld_t = p.at("ls")

                    @block.vector
                    def _(v, wdst=wdst, c=c, w=w, half=half, hw=hw, ld_t=ld_t):
                        v.wait_ge(p.sem["ls"], ld_t)
                        p.inc(v.tensor_copy(wdst[:, c * w + half:c * w + half + hw], wtmp[:, 0:hw]), "dv")

        for i in range(16):
            buf = xt[i % 2]

            @block.sync
            def _(sync, i=i, buf=buf):
                sync.wait_ge(p.sem["dv"], p.at("dv"))
                sync.dma_start(out=buf[:], in_=x1[i * 128:(i + 1) * 128, :]).then_inc(p.sem["ls"], 16)
                p.cnt["ls"] += 16
            ld_t = p.at("ls")

            @block.vector
            def _(v, ld_t=ld_t):
                v.wait_ge(p.sem["ls"], ld_t)
                v.wait_ge(p.sem["st"], p.at("st"))

            @block.scalar
            def _(s, ld_t=ld_t):
                s.wait_ge(p.sem["ls"], ld_t)

            emit_ln(p, block, buf[:], [xn1t[:], xn1b[:]], stats, x2)
            emit_ln(p, block, xn1t[:], [nrmt[:]], stats, x2)

            @block.sync
            def _(sync, i=i):
                sync.wait_ge(p.sem["dv"], p.at("dv"))
                sync.dma_start(out=xn1_o[i * 128:(i + 1) * 128, :], in_=xn1t[:]).then_inc(p.sem["st"], 16)
                sync.dma_start(out=xn1b_d[i * 128:(i + 1) * 128, :], in_=xn1b[:]).then_inc(p.sem["st"], 16)
                sync.dma_start(out=nrm_tm_d[i * 128:(i + 1) * 128, :], in_=nrmt[:]).then_inc(p.sem["st"], 16)
                p.cnt["st"] += 48

        all_st = p.at("st")

        for d in range(8):
            for tb in range(4):
                @block.scalar
                def _(s, d=d, tb=tb):
                    idx = d * 4 + tb
                    s.wait_ge(p.sem["st"], all_st + (16 * (idx - 1) if idx >= 2 else 0))
                    s.dma_start_transpose(tpose[(d * 4 + tb) % 2][:],
                                          nrm_tm_d[tb * 512:(tb + 1) * 512, d * 128:(d + 1) * 128]
                                          ).then_inc(p.sem["lx"], 16)
                    p.cnt["lx"] += 16

                @block.vector
                def _(v, d=d, tb=tb):
                    v.wait_ge(p.sem["lx"], p.at("lx"))
                    v.wait_ge(p.sem["ls"], lncols_ld)
                    v.wait_ge(p.sem["st"], p.at("st"))
                    p.inc(v.tensor_scalar(tpose[(d * 4 + tb) % 2][:], tpose[(d * 4 + tb) % 2][:],
                                          lncols[:, d:d + 1], lncols[:, 8 + d:9 + d],
                                          ALU.mult, ALU.add), "dv")

                @block.sync
                def _(sync, d=d, tb=tb):
                    sync.wait_ge(p.sem["dv"], p.at("dv"))
                    sync.dma_start(out=nrm_o[d * 128:(d + 1) * 128, tb * 512:(tb + 1) * 512],
                                   in_=tpose[(d * 4 + tb) % 2][:]).then_inc(p.sem["st"], 16)
                    p.cnt["st"] += 16

        for tb in range(4):
            @block.scalar
            def _(s, tb=tb):
                s.wait_ge(p.sem["te"], p.at("te"))
                for d in range(8):
                    s.dma_start_transpose(xnf_t[:, d * 512:(d + 1) * 512],
                                          xn1b_d[tb * 512:(tb + 1) * 512, d * 128:(d + 1) * 128]
                                          ).then_inc(p.sem["lx"], 16)
                    p.cnt["lx"] += 16
            xld = p.at("lx")

            for fc in range(32):
                @block.tensor
                def _(t, tb=tb, fc=fc, xld=xld):
                    t.wait_ge(p.sem["lx"], xld)
                    t.wait_ge(p.sem["dv"], p.at("dv"))
                    bank = ps[fc % 2]
                    ins = None
                    for d in range(8):
                        ins = t.matmul(bank[:],
                                       wfc[:, d * 4096 + fc * 128:d * 4096 + (fc + 1) * 128],
                                       xnf_t[:, d * 512:(d + 1) * 512],
                                       start=(d == 0), stop=(d == 7))
                    p.inc(ins, "te")

                @block.scalar
                def _(s, tb=tb, fc=fc):
                    s.wait_ge(p.sem["te"], p.at("te"))
                    s.wait_ge(p.sem["dv"], p.at("dv"))
                    bank = ps[fc % 2]
                    p.inc(s.copy(hev[:], bank[:]), "ac")

                @block.vector
                def _(v, tb=tb, fc=fc):
                    v.wait_ge(p.sem["ac"], p.at("ac"))
                    p.inc(v.scalar_tensor_tensor(h2[:, fc * 512:(fc + 1) * 512], hev[:], 0.0,
                                                 hev[:], ALU.max, ALU.mult), "dv")

            h2_dv = p.at("dv")

            for ts_ in range(4):
                for dh in range(2):
                    @block.tensor
                    def _(t, tb=tb, ts_=ts_, dh=dh):
                        t.wait_ge(p.sem["dv"], h2_dv)
                        t.wait_ge(p.sem["gp"], p.at("gp"))
                        bank = ps[2 + dh]
                        ins = None
                        for fc in range(32):
                            ins = t.matmul(bank[:],
                                           h2[:, fc * 512 + ts_ * 128:fc * 512 + (ts_ + 1) * 128],
                                           wpr[:, fc * 1024 + dh * 512:fc * 1024 + (dh + 1) * 512],
                                           start=(fc == 0), stop=(fc == 31))
                        p.inc(ins, "te")

                    @block.scalar
                    def _(s, tb=tb, ts_=ts_, dh=dh):
                        s.wait_ge(p.sem["te"], p.at("te"))
                        s.wait_ge(p.sem["gp"], p.at("gp"))
                        bank = ps[2 + dh]
                        p.inc(s.copy(yse[:], bank[:]), "ac")

                    @block.gpsimd
                    def _(g, tb=tb, ts_=ts_, dh=dh):
                        g.wait_ge(p.sem["ac"], p.at("ac"))
                        g.tensor_scalar(sacc[:], yse[:], SNN_T[0], SNN_D[0], ALU.is_gt, ALU.mult)
                        for j in range(1, 5):
                            g.tensor_scalar(smsk[:], yse[:], SNN_T[j], SNN_D[j], ALU.is_gt, ALU.mult)
                            g.tensor_tensor(sacc[:], sacc[:], smsk[:], ALU.add)
                        p.inc(g.engine_nop(), "gp")

                    @block.sync
                    def _(sync, tb=tb, ts_=ts_, dh=dh):
                        sync.wait_ge(p.sem["gp"], p.at("gp"))
                        sync.dma_start(out=g1_o[tb * 512 + ts_ * 128:tb * 512 + (ts_ + 1) * 128,
                                                dh * 512:(dh + 1) * 512],
                                       in_=sacc[:]).then_inc(p.sem["st"], 16)
                        p.cnt["st"] += 16

                    @block.gpsimd
                    def _(g, tb=tb, ts_=ts_, dh=dh):
                        g.wait_ge(p.sem["st"], p.at("st"))
                        p.inc(g.engine_nop(), "gp")

        @block.sync
        def _(sync):
            sync.wait_ge(p.sem["st"], p.at("st"))

    return p.nc


def build_launch_E(dbg=False):
    p = P()
    nc = p.nc
    DK = "ExternalOutput" if dbg else "Internal"
    nrm = nc.dram_tensor("nrm_fm", [1024, 8192], BF16, kind="ExternalInput").ap()
    wqh = nc.dram_tensor("wqh", [1024, 256], F32, kind="ExternalInput").ap()
    wkh = nc.dram_tensor("wkh", [1024, 256], F32, kind="ExternalInput").ap()
    wvh = nc.dram_tensor("wvh", [1024, 512], F32, kind="ExternalInput").ap()
    wgh = nc.dram_tensor("wgh", [1024, 512], F32, kind="ExternalInput").ap()
    wbh = nc.dram_tensor("wbh", [1024, 2], F32, kind="ExternalInput").ap()
    convq = nc.dram_tensor("convq", [256, 4], F32, kind="ExternalInput").ap()
    convk = nc.dram_tensor("convk", [256, 4], F32, kind="ExternalInput").ap()
    convv = nc.dram_tensor("convv", [512, 4], F32, kind="ExternalInput").ap()
    scal = nc.dram_tensor("scal", [1, 2], F32, kind="ExternalInput").ap()
    normw = nc.dram_tensor("normw", [1, 512], F32, kind="ExternalInput").ap()
    og_o = nc.dram_tensor("og_tm", [8192, 512], BF16, kind="ExternalOutput").ap()
    qraw_d = nc.dram_tensor("qraw_scr", [256, 8192], BF16, kind=DK).ap()
    kraw_d = nc.dram_tensor("kraw_scr", [256, 8192], BF16, kind=DK).ap()
    vraw_d = nc.dram_tensor("vraw_scr", [512, 8192], BF16, kind=DK).ap()
    kpost_d = nc.dram_tensor("kpost_scr", [256, 8192], BF16, kind=DK).ap()
    vpost_d = nc.dram_tensor("vpost_scr", [512, 8192], BF16, kind=DK).ap()
    gout_d = nc.dram_tensor("gout_scr", [8192, 512], BF16, kind=DK).ap()

    q_fm = p.sbuf("q_fmE", [128, 2 * 8192], BF16)
    k_fm = p.sbuf("k_fmE", [128, 2 * 8192], BF16)
    ktc = [p.sbuf(f"ktcE{i}", [128, 256], BF16) for i in range(2)]
    vtc = [p.sbuf(f"vtcE{i}", [128, 512], BF16) for i in range(2)]
    nrmb = p.sbuf("nrmbE", [128, 8 * 512], BF16)
    wqb = p.sbuf("wqbE", [128, 8 * 256], BF16)
    wkb = p.sbuf("wkbE", [128, 8 * 256], BF16)
    wvb = p.sbuf("wvbE", [128, 8 * 512], BF16)
    wgb = p.sbuf("wgbE", [128, 8 * 512], BF16)
    wbb = p.sbuf("wbbE", [128, 8 * 2], BF16)
    wtmp = p.sbuf("wtmpE", [128, 512], F32)
    cvw = p.sbuf("cvwE", [128, 32], F32)
    convin = p.sbuf("convinE", [128, 520], BF16)
    cacc = p.sbuf("caccE", [128, 512], F32)
    silo = [p.sbuf(f"siloE{i}", [128, 512], F32) for i in range(4)]
    sqb = p.sbuf("sqbE", [128, 512], BF16)
    rowA = p.sbuf("rowAE", [1, 520], F32)
    rowB = p.sbuf("rowBE", [1, 512], BF16)
    ones128 = p.sbuf("ones128E", [128, 1], BF16)
    ones1 = p.sbuf("ones1E", [1, 128], BF16)
    ones1f = p.sbuf("ones1fE", [1, 128], F32)
    evac = p.sbuf("evacE", [128, 512], BF16)
    tbl = {k: p.sbuf(f"tbl_{k}", [128, 64], F32) for k in
           ["braw", "araw", "lnb", "beta", "gdec", "s", "Es", "EsCs", "PC", "bEs", "sct", "tmp"]}
    cm_g = p.sbuf("cm_g", [64, 128], F32)
    cm_s = p.sbuf("cm_s", [64, 128], F32)
    cm_one = p.sbuf("cm_one", [64, 128], F32)
    sC_cm = p.sbuf("sC_cm", [64, 1], F32)
    eA = p.sbuf("eAE", [128, 1], F32)
    dtb = p.sbuf("dtbE", [128, 1], F32)
    cst1 = p.sbuf("cst1E", [128, 1], F32)
    scalb = p.sbuf("scalbE", [1, 2], BF16)
    tiny = p.sbuf("tinyE", [128, 8], F32)
    msku = p.sbuf("mskuE", [128, 128], F32)
    mskl = p.sbuf("msklE", [128, 128], F32)
    mski = p.sbuf("mskiE", [128, 128], F32)
    itile = p.sbuf("itileE", [128, 128], BF16)
    itf = p.sbuf("itfE", [128, 128], F32)
    io1 = p.sbuf("io1E", [128, 128], I32)
    io2 = p.sbuf("io2E", [128, 128], I32)
    normw_rep = p.sbuf("normw_repE", [128, 512], F32)
    normw_b = p.sbuf("normw_bE", [1, 512], BF16)
    z3 = p.sbuf("z3E", [128, 128], F32)
    za = p.sbuf("zaE", [128, 128], F32)
    zb = p.sbuf("zbE", [128, 128], F32)
    lnbm = p.sbuf("lnbmE", [128, 128], F32)
    A1 = p.sbuf("A1E", [128, 128], BF16)
    A1T = p.sbuf("A1TE", [128, 128], BF16)
    Aqk = p.sbuf("AqkE", [128, 128], BF16)
    Gt = p.sbuf("GtE", [128, 128], BF16)
    Xt_ = p.sbuf("Xt_E", [128, 128], BF16)
    XTt = p.sbuf("XTtE", [128, 128], BF16)
    WtTn = p.sbuf("WtTnE", [128, 2 * 128], BF16)
    bV = p.sbuf("bVE", [128, 512], BF16)
    bsK = p.sbuf("bsKE", [128, 256], BF16)
    U_bf = p.sbuf("U_bfE", [128, 512], BF16)
    Ubar = p.sbuf("UbarE", [128, 512], BF16)
    S_f = [p.sbuf(f"S_fE{i}", [128, 512], F32) for i in range(2)]
    Sb = [p.sbuf(f"SbE{i}", [128, 512], BF16) for i in range(2)]
    o_in = p.sbuf("o_inE", [128, 512], F32)
    o_raw = p.sbuf("o_rawE", [128, 512], F32)
    o1t = p.sbuf("o1E", [128, 512], F32)
    gt_t = p.sbuf("gt_tE", [128, 512], BF16)
    sgt = p.sbuf("sgtE", [128, 512], F32)
    ggt = p.sbuf("ggtE", [128, 512], F32)
    og_t = p.sbuf("og_tE", [128, 512], BF16)
    pS1 = p.psum("pS1E", [128, 512], F32)
    pS2 = p.psum("pS2E", [128, 512], F32)
    pU = p.psum("pUE", [128, 512], F32)
    pSu = [p.psum(f"pSuE{i}", [128, 512], F32) for i in range(2)]
    pOi = p.psum("pOiE", [128, 512], F32)
    pOe = p.psum("pOeE", [128, 512], F32)
    pR = p.psum("pRE", [128, 128], F32)

    for s in ["ls", "lx", "dv", "gp", "ac", "te", "st"]:
        p.semaphore(s)

    W = {n: v for n, v in locals().items() if n not in ("p", "nc", "s")}

    with p.ctx, nc.Block() as block:
        @block.gpsimd
        def _(g):
            g.iota(io1[:], [[0, 128]], channel_multiplier=1)
            p.inc(g.iota(io2[:], [[1, 128]], channel_multiplier=0), "gp")

        @block.sync
        def _(sync):
            sync.dma_start(out=rowA[0:1, 0:2], in_=scal[:]).then_inc(p.sem["ls"], 16)
            sync.dma_start(out=rowA[0:1, 8:520], in_=normw[:]).then_inc(p.sem["ls"], 16)
            sync.dma_start(out=cvw[:, 0:8].rearrange("p (c k) -> p c k", k=4),
                           in_=convq.rearrange("(c p) k -> p c k", p=128),
                           allow_slow_non_contiguous=True).then_inc(p.sem["ls"], 16)
            sync.dma_start(out=cvw[:, 8:16].rearrange("p (c k) -> p c k", k=4),
                           in_=convk.rearrange("(c p) k -> p c k", p=128),
                           allow_slow_non_contiguous=True).then_inc(p.sem["ls"], 16)
            sync.dma_start(out=cvw[:, 16:32].rearrange("p (c k) -> p c k", k=4),
                           in_=convv.rearrange("(c p) k -> p c k", p=128),
                           allow_slow_non_contiguous=True).then_inc(p.sem["ls"], 16)
            p.cnt["ls"] += 80

        @block.vector
        def _(v):
            v.wait_ge(p.sem["gp"], p.at("gp"))
            v.wait_ge(p.sem["ls"], p.at("ls"))
            v.memset(ones128[:], 1.0)
            v.memset(ones1[:], 1.0)
            v.memset(ones1f[:], 1.0)
            v.memset(cst1[:], 1.0)
            v.memset(cm_one[:], 1.0)
            v.tensor_tensor(itile[:], io1[:], io2[:], ALU.is_equal)
            v.tensor_tensor(itf[:], io1[:], io2[:], ALU.is_equal)
            v.tensor_tensor(msku[:], io2[:], io1[:], ALU.is_gt)
            v.tensor_scalar(msku[:], msku[:], 1.0, 1e30, ALU.subtract, ALU.mult)
            v.tensor_tensor(mskl[:], io1[:], io2[:], ALU.is_gt)
            v.tensor_scalar(mskl[:], mskl[:], 1.0, -1e30, ALU.subtract, ALU.mult)
            v.tensor_tensor(mski[:], io2[:], io1[:], ALU.is_ge)
            v.tensor_scalar(mski[:], mski[:], 1.0, 1e30, ALU.subtract, ALU.mult)
            v.tensor_copy(scalb[:], rowA[0:1, 0:2])
            p.inc(v.tensor_copy(normw_b[:], rowA[0:1, 8:520]), "dv")

        @block.tensor
        def _(t):
            t.wait_ge(p.sem["dv"], p.at("dv"))
            t.matmul(pR[:, 0:1], ones1[:], scalb[0:1, 1:2], start=True, stop=True)
            t.matmul(pR[:, 1:2], ones1[:], scalb[0:1, 0:1], start=True, stop=True)
            p.inc(t.matmul(pS1[:], ones1[:], normw_b[:], start=True, stop=True), "te")

        @block.vector
        def _(v):
            v.wait_ge(p.sem["te"], p.at("te"))
            v.tensor_copy(dtb[:], pR[:, 0:1])
            v.tensor_copy(tiny[:, 0:1], pR[:, 1:2])
            p.inc(v.tensor_copy(normw_rep[:], pS1[:]), "dv")

        @block.scalar
        def _(s):
            s.wait_ge(p.sem["dv"], p.at("dv"))
            p.inc(s.activation(eA[:], tiny[:, 0:1], AF.Exp), "ac")

        for (wsrc_d, wdst, w) in [(wqh, wqb, 256), (wkh, wkb, 256), (wvh, wvb, 512),
                                  (wgh, wgb, 512), (wbh, wbb, 2)]:
            for c in range(8):
                @block.sync
                def _(sync, wsrc_d=wsrc_d, c=c, w=w):
                    sync.wait_ge(p.sem["dv"], p.at("dv"))
                    sync.dma_start(out=wtmp[:, 0:w], in_=wsrc_d[c * 128:(c + 1) * 128, :]).then_inc(p.sem["ls"], 16)
                    p.cnt["ls"] += 16
                ld_t = p.at("ls")

                @block.vector
                def _(v, wdst=wdst, c=c, w=w, ld_t=ld_t):
                    v.wait_ge(p.sem["ls"], ld_t)
                    p.inc(v.tensor_copy(wdst[:, c * w:(c + 1) * w], wtmp[:, 0:w]), "dv")

        for tb in range(16):
            @block.sync
            def _(sync, tb=tb):
                sync.wait_ge(p.sem["te"], p.at("te"))
                for d in range(8):
                    sync.dma_start(out=nrmb[:, d * 512:(d + 1) * 512],
                                   in_=nrm[d * 128:(d + 1) * 128, tb * 512:(tb + 1) * 512]
                                   ).then_inc(p.sem["ls"], 16)
                    p.cnt["ls"] += 16
            nld = p.at("ls")

            for (wb_, outd, ne) in [(wqb, qraw_d, 2), (wkb, kraw_d, 2), (wvb, vraw_d, 4)]:
                for e in range(ne):
                    @block.tensor
                    def _(t, tb=tb, wb_=wb_, e=e, ne=ne, nld=nld):
                        t.wait_ge(p.sem["ls"], nld)
                        t.wait_ge(p.sem["dv"], p.at("dv"))
                        w = ne * 128
                        ins = None
                        for d in range(8):
                            ins = t.matmul(pS1[:],
                                           wb_[:, d * w + e * 128:d * w + (e + 1) * 128],
                                           nrmb[:, d * 512:(d + 1) * 512],
                                           start=(d == 0), stop=(d == 7))
                        p.inc(ins, "te")

                    @block.vector
                    def _(v):
                        v.wait_ge(p.sem["te"], p.at("te"))
                        v.wait_ge(p.sem["st"], p.at("st"))
                        p.inc(v.tensor_copy(evac[:], pS1[:]), "dv")

                    @block.sync
                    def _(sync, tb=tb, outd=outd, e=e):
                        sync.wait_ge(p.sem["dv"], p.at("dv"))
                        sync.dma_start(out=outd[e * 128:(e + 1) * 128, tb * 512:(tb + 1) * 512],
                                       in_=evac[:]).then_inc(p.sem["st"], 16)
                        p.cnt["st"] += 16

            for ts_ in range(4):
                @block.tensor
                def _(t, tb=tb, ts_=ts_, nld=nld):
                    t.wait_ge(p.sem["ls"], nld)
                    t.wait_ge(p.sem["ac"], p.at("ac"))
                    t.wait_ge(p.sem["dv"], p.at("dv"))
                    ins = None
                    for d in range(8):
                        ins = t.matmul(pOe[:],
                                       nrmb[:, d * 512 + ts_ * 128:d * 512 + (ts_ + 1) * 128],
                                       wgb[:, d * 512:(d + 1) * 512],
                                       start=(d == 0), stop=(d == 7))
                    p.inc(ins, "te")
                    ins = None
                    for d in range(8):
                        ins = t.matmul(pR[:, 0:2],
                                       nrmb[:, d * 512 + ts_ * 128:d * 512 + (ts_ + 1) * 128],
                                       wbb[:, d * 2:(d + 1) * 2],
                                       start=(d == 0), stop=(d == 7))
                    p.inc(ins, "te")

                @block.scalar
                def _(s, tb=tb, ts_=ts_):
                    s.wait_ge(p.sem["te"], p.at("te"))
                    s.wait_ge(p.sem["st"], p.at("st"))
                    p.inc(s.copy(og_t[:], pOe[:]), "ac")

                @block.vector
                def _(v, tb=tb, ts_=ts_):
                    v.wait_ge(p.sem["te"], p.at("te"))
                    c = tb * 4 + ts_
                    v.tensor_copy(tbl["braw"][:, c:c + 1], pR[:, 0:1])
                    p.inc(v.tensor_copy(tbl["araw"][:, c:c + 1], pR[:, 1:2]), "dv")

                @block.sync
                def _(sync, tb=tb, ts_=ts_):
                    sync.wait_ge(p.sem["ac"], p.at("ac"))
                    sync.dma_start(out=gout_d[tb * 512 + ts_ * 128:tb * 512 + (ts_ + 1) * 128, :],
                                   in_=og_t[:]).then_inc(p.sem["st"], 16)
                    p.cnt["st"] += 16

        proj_st = p.at("st")

        def conv_stream(raw_d, nch, cwoff, l2, dst_sbuf, dst_dram, qscale):
            for blk in range(16):
                for c in range(nch):
                    @block.sync
                    def _(sync, raw_d=raw_d, c=c, blk=blk):
                        sync.wait_ge(p.sem["dv"], p.at("dv"))
                        if blk == 0:
                            sync.dma_start(out=convin[:, 3:515],
                                           in_=raw_d[c * 128:(c + 1) * 128, 0:512]).then_inc(p.sem["ls"], 16)
                        else:
                            sync.dma_start(out=convin[:, 0:515],
                                           in_=raw_d[c * 128:(c + 1) * 128, blk * 512 - 3:(blk + 1) * 512]
                                           ).then_inc(p.sem["ls"], 16)
                        p.cnt["ls"] += 16
                    cld = p.at("ls")

                    @block.vector
                    def _(v, c=c, blk=blk, cld=cld, cwoff=cwoff):
                        v.wait_ge(p.sem["ls"], cld)
                        v.wait_ge(p.sem["ac"], p.at("ac"))
                        if blk == 0:
                            v.memset(convin[:, 0:3], 0.0)
                        cw = cvw[:, cwoff + c * 4:cwoff + (c + 1) * 4]
                        v.tensor_scalar(cacc[:], convin[:, 0:512], cw[:, 0:1], None, ALU.mult)
                        for j in range(1, 4):
                            v.scalar_tensor_tensor(cacc[:], convin[:, j:j + 512], cw[:, j:j + 1],
                                                   cacc[:], ALU.mult, ALU.add)
                        p.inc(v.engine_nop(), "dv")

                    @block.scalar
                    def _(s, c=c, blk=blk):
                        s.wait_ge(p.sem["dv"], p.at("dv"))
                        s.wait_ge(p.sem["te"], p.at("te"))
                        p.inc(s.activation(silo[c][:], cacc[:], AF.Silu), "ac")

                if l2:
                    for c in range(nch):
                        @block.scalar
                        def _(s, c=c):
                            s.wait_ge(p.sem["ac"], p.at("ac"))
                            s.wait_ge(p.sem["te"], p.at("te"))
                            p.inc(s.activation(sqb[:], silo[c][:], AF.Square), "ac")

                        @block.tensor
                        def _(t, c=c, nch=nch):
                            t.wait_ge(p.sem["ac"], p.at("ac"))
                            if c == 0:
                                t.wait_ge(p.sem["dv"], p.at("dv"))
                            p.inc(t.matmul(pSu[0][0:1, 0:512], ones128[:], sqb[:],
                                           start=(c == 0), stop=(c == nch - 1)), "te")

                    @block.vector
                    def _(v):
                        v.wait_ge(p.sem["te"], p.at("te"))
                        p.inc(v.tensor_single_scalar(rowA[0:1, 0:512], pSu[0][0:1, 0:512],
                                                     1e-6, ALU.add), "dv")
                        v.wait_ge(p.sem["dv"], p.at("dv"))
                        p.inc(v.reciprocal(rowA[0:1, 0:512], rowA[0:1, 0:512]), "dv")

                    @block.scalar
                    def _(s):
                        s.wait_ge(p.sem["dv"], p.at("dv"))
                        p.inc(s.activation(rowA[0:1, 0:512], rowA[0:1, 0:512], AF.Sqrt), "ac")

                    @block.vector
                    def _(v, qscale=qscale):
                        v.wait_ge(p.sem["ac"], p.at("ac"))
                        if qscale != 1.0:
                            v.tensor_scalar_mul(rowA[0:1, 0:512], rowA[0:1, 0:512], qscale)
                        p.inc(v.tensor_copy(rowB[0:1, 0:512], rowA[0:1, 0:512]), "dv")

                    @block.tensor
                    def _(t):
                        t.wait_ge(p.sem["dv"], p.at("dv"))
                        p.inc(t.matmul(pSu[1][:], ones1[:], rowB[0:1, 0:512], start=True, stop=True), "te")

                for c in range(nch):
                    @block.vector
                    def _(v, c=c, blk=blk, l2=l2, dst_sbuf=dst_sbuf):
                        v.wait_ge(p.sem["te"], p.at("te"))
                        v.wait_ge(p.sem["st"], p.at("st"))
                        if l2:
                            ins = v.tensor_tensor(evac[:], silo[c][:], pSu[1][:], ALU.mult)
                        else:
                            ins = v.tensor_copy(evac[:], silo[c][:])
                        if dst_sbuf is not None:
                            ins = v.tensor_copy(dst_sbuf(c, blk), evac[:])
                        p.inc(ins, "dv")

                    @block.sync
                    def _(sync, c=c, blk=blk, dst_dram=dst_dram):
                        sync.wait_ge(p.sem["dv"], p.at("dv"))
                        if dst_dram is not None:
                            sync.dma_start(out=dst_dram[c * 128:(c + 1) * 128, blk * 512:(blk + 1) * 512],
                                           in_=evac[:]).then_inc(p.sem["st"], 16)
                            p.cnt["st"] += 16

        @block.sync
        def _(sync):
            sync.wait_ge(p.sem["st"], proj_st)

        conv_stream(qraw_d, 2, 0, True,
                    lambda c, blk: q_fm[:, c * 8192 + blk * 512:c * 8192 + (blk + 1) * 512], None, 0.0625)
        conv_stream(kraw_d, 2, 8, True,
                    lambda c, blk: k_fm[:, c * 8192 + blk * 512:c * 8192 + (blk + 1) * 512], kpost_d, 1.0)
        conv_stream(vraw_d, 4, 16, False, None, vpost_d, 1.0)

        kv_st = p.at("st")
        p.kvlx = {}

        @block.scalar
        def _(s):
            s.wait_ge(p.sem["st"], kv_st)
            s.dma_start_transpose(ktc[0][:], kpost_d[:, 0:128]).then_inc(p.sem["lx"], 16)
            s.dma_start_transpose(vtc[0][:], vpost_d[:, 0:128]).then_inc(p.sem["lx"], 16)
            p.cnt["lx"] += 32
        p.kvlx[0] = p.at("lx")
        kv_lx = p.at("lx")

        def act_step(dst, src, func, scale=1.0, bias=None):
            @block.scalar
            def _(s):
                s.wait_ge(p.sem["ac"], p.at("ac"))
                s.wait_ge(p.sem["dv"], p.at("dv"))
                kw = {}
                if bias is not None:
                    kw["bias"] = bias
                p.inc(s.activation(dst, src, func, scale=scale, **kw), "ac")

        def dve_step(fn):
            @block.vector
            def _(v):
                v.wait_ge(p.sem["ac"], p.at("ac"))
                v.wait_ge(p.sem["te"], p.at("te"))
                v.wait_ge(p.sem["dv"], p.at("dv"))
                p.inc(fn(v), "dv")

        act_step(tbl["lnb"][:], tbl["braw"][:], AF.Abs)
        act_step(tbl["lnb"][:], tbl["lnb"][:], AF.Exp, scale=-1.0)
        act_step(tbl["lnb"][:], tbl["lnb"][:], AF.Ln, bias=cst1[:, 0:1])
        dve_step(lambda v: v.tensor_single_scalar(tbl["sct"][:], tbl["braw"][:], 0.0, ALU.min))
        dve_step(lambda v: v.tensor_tensor(tbl["lnb"][:], tbl["sct"][:], tbl["lnb"][:], ALU.subtract))
        act_step(tbl["beta"][:], tbl["braw"][:], AF.Sigmoid)
        dve_step(lambda v: v.tensor_scalar(tbl["sct"][:], tbl["araw"][:], dtb[:, 0:1], None, ALU.add))
        act_step(tbl["gdec"][:], tbl["sct"][:], AF.Abs)
        act_step(tbl["gdec"][:], tbl["gdec"][:], AF.Exp, scale=-1.0)
        act_step(tbl["gdec"][:], tbl["gdec"][:], AF.Ln, bias=cst1[:, 0:1])
        dve_step(lambda v: v.tensor_single_scalar(tbl["tmp"][:], tbl["sct"][:], 0.0, ALU.max))
        dve_step(lambda v: v.tensor_tensor(tbl["tmp"][:], tbl["tmp"][:], tbl["gdec"][:], ALU.add))
        dve_step(lambda v: v.tensor_scalar(tbl["gdec"][:], tbl["tmp"][:], eA[:, 0:1], -1.0,
                                           ALU.mult, ALU.mult))

        @block.tensor
        def _(t):
            t.wait_ge(p.sem["dv"], p.at("dv"))
            p.inc(t.transpose(pR[0:64, 0:128], tbl["gdec"][:, 0:64], itf[:]), "te")

        dve_step(lambda v: v.tensor_copy(cm_g[:], pR[0:64, 0:128]))
        dve_step(lambda v: v.tensor_tensor_scan(cm_s[:], cm_one[:], cm_g[:], 0.0,
                                                ALU.mult, ALU.add))

        @block.tensor
        def _(t):
            t.wait_ge(p.sem["dv"], p.at("dv"))
            p.inc(t.transpose(pR[:, 0:64], cm_s[0:64, :], itf[0:64, 0:64]), "te")

        dve_step(lambda v: v.tensor_copy(tbl["s"][:], pR[:, 0:64]))
        dve_step(lambda v: v.tensor_copy(sC_cm[:], cm_s[0:64, 127:128]))

        @block.tensor
        def _(t):
            t.wait_ge(p.sem["dv"], p.at("dv"))
            p.inc(t.transpose(pR[0:1, 0:64], sC_cm[0:64, 0:1], itf[0:64, 0:64]), "te")

        dve_step(lambda v: v.tensor_copy(rowA[0:1, 0:64], pR[0:1, 0:64]))

        @block.tensor
        def _(t):
            t.wait_ge(p.sem["dv"], p.at("dv"))
            p.inc(t.matmul(pR[:, 0:64], ones1f[:], rowA[0:1, 0:64], start=True, stop=True), "te")

        dve_step(lambda v: v.tensor_copy(tbl["sct"][:], pR[:, 0:64]))
        dve_step(lambda v: v.tensor_tensor(tbl["EsCs"][:], tbl["sct"][:], tbl["s"][:], ALU.subtract))
        dve_step(lambda v: v.tensor_tensor(tbl["bEs"][:], tbl["s"][:], tbl["lnb"][:], ALU.add))

        act_step(tbl["Es"][:], tbl["s"][:], AF.Exp)
        act_step(tbl["EsCs"][:], tbl["EsCs"][:], AF.Exp)
        act_step(tbl["PC"][:], tbl["sct"][:], AF.Exp)
        act_step(tbl["bEs"][:], tbl["bEs"][:], AF.Exp)

        dve_step(lambda v: (v.memset(S_f[0][:], 0.0), v.memset(S_f[1][:], 0.0))[-1])

        if dbg:
            tbl_o = nc.dram_tensor("tbl_dbg", [128, 64 * 12], F32, kind="ExternalOutput").ap()
            qfm_o = nc.dram_tensor("qfm_dbg", [128, 2 * 8192], BF16, kind="ExternalOutput").ap()
            kfm_o = nc.dram_tensor("kfm_dbg", [128, 2 * 8192], BF16, kind="ExternalOutput").ap()

            @block.sync
            def _(sync):
                sync.wait_ge(p.sem["ac"], p.at("ac"))
                sync.wait_ge(p.sem["dv"], p.at("dv"))
                for ki, k in enumerate(["braw", "araw", "lnb", "beta", "gdec", "s",
                                        "Es", "EsCs", "PC", "bEs", "sct", "tmp"]):
                    sync.dma_start(out=tbl_o[:, ki * 64:(ki + 1) * 64], in_=tbl[k][:]).then_inc(p.sem["st"], 16)
                    p.cnt["st"] += 16
                sync.dma_start(out=qfm_o[:], in_=q_fm[:]).then_inc(p.sem["st"], 16)
                sync.dma_start(out=kfm_o[:], in_=k_fm[:]).then_inc(p.sem["st"], 16)
                p.cnt["st"] += 32

        if dbg:
            c0_o = nc.dram_tensor("c0_dbg", [128, 3584], F32, kind="ExternalOutput").ap()
            W["c0_o"] = c0_o
        W["dbg"] = dbg

        for c in range(64):
            _emit_chunk_E(p, block, c, W, kv_lx)

        @block.sync
        def _(sync):
            sync.wait_ge(p.sem["st"], p.at("st"))

    return p.nc


def _emit_chunk_E(p, block, c, W, kv_lx):
    q_fm = W["q_fm"]; k_fm = W["k_fm"]; ktc = W["ktc"]; vtc = W["vtc"]
    kpost_d = W["kpost_d"]; vpost_d = W["vpost_d"]
    tbl = W["tbl"]; msku = W["msku"]; mskl = W["mskl"]; mski = W["mski"]
    itf = W["itf"]; ones1f = W["ones1f"]; z3 = W["z3"]; za = W["za"]; zb = W["zb"]
    lnbm = W["lnbm"]; A1 = W["A1"]; A1T = W["A1T"]; Aqk = W["Aqk"]; Gt = W["Gt"]
    Xt_ = W["Xt_"]; XTt = W["XTt"]; WtTn = W["WtTn"]; bV = W["bV"]; bsK = W["bsK"]
    U_bf = W["U_bf"]; Ubar = W["Ubar"]; S_f = W["S_f"]; Sb = W["Sb"]
    o_in = W["o_in"]; o_raw = W["o_raw"]; o1t = W["o1t"]; gt_t = W["gt_t"]
    sgt = W["sgt"]; ggt = W["ggt"]; og_t = W["og_t"]
    pS1 = W["pS1"]; pS2 = W["pS2"]; pU = W["pU"]; pSu = W["pSu"]
    pOi = W["pOi"]; pOe = W["pOe"]; pR = W["pR"]
    normw_rep = W["normw_rep"]; gout_d = W["gout_d"]; og_o = W["og_o"]
    rowA = W["rowA"]; evac = W["evac"]; itile = W["itile"]

    @block.sync
    def _(sync):
        sync.wait_ge(p.sem["gp"], p.at("gp"))
        sync.dma_start(out=gt_t[:], in_=gout_d[c * 128:(c + 1) * 128, :]).then_inc(p.sem["ls"], 16)
        p.cnt["ls"] += 16
    gld = p.at("ls")

    if c + 1 < 64:
        @block.scalar
        def _(s):
            s.wait_ge(p.sem["te"], p.at("te"))
            s.dma_start_transpose(ktc[(c + 1) % 2][:],
                                  kpost_d[:, (c + 1) * 128:(c + 2) * 128]).then_inc(p.sem["lx"], 16)
            s.dma_start_transpose(vtc[(c + 1) % 2][:],
                                  vpost_d[:, (c + 1) * 128:(c + 2) * 128]).then_inc(p.sem["lx"], 16)
            p.cnt["lx"] += 32
        p.kvlx[c + 1] = p.at("lx")

    @block.tensor
    def _(t):
        t.wait_ge(p.sem["dv"], p.at("dv"))
        t.wait_ge(p.sem["lx"], kv_lx)
        p.inc(t.transpose(pR[0:1, 0:128], tbl["s"][:, c:c + 1], itf[:]), "te")

    @block.vector
    def _(v):
        v.wait_ge(p.sem["te"], p.at("te"))
        p.inc(v.tensor_copy(rowA[0:1, 0:128], pR[0:1, 0:128]), "dv")

    @block.tensor
    def _(t):
        t.wait_ge(p.sem["dv"], p.at("dv"))
        p.inc(t.transpose(pR[0:1, 0:128], tbl["lnb"][:, c:c + 1], itf[:]), "te")

    @block.vector
    def _(v):
        v.wait_ge(p.sem["te"], p.at("te"))
        p.inc(v.tensor_copy(rowA[0:1, 128:256], pR[0:1, 0:128]), "dv")

    @block.tensor
    def _(t):
        t.wait_ge(p.sem["dv"], p.at("dv"))
        t.matmul(pS1[:, 0:128], ones1f[:], rowA[0:1, 0:128].bitcast(F32), start=True, stop=True)
        t.matmul(pS1[:, 128:256], ones1f[:], rowA[0:1, 128:256].bitcast(F32), start=True, stop=True)
        for d in range(2):
            t.matmul(pS1[:, 256:384],
                     k_fm[:, d * 8192 + c * 128:d * 8192 + (c + 1) * 128],
                     k_fm[:, d * 8192 + c * 128:d * 8192 + (c + 1) * 128],
                     start=(d == 0), stop=(d == 1))
        for d in range(2):
            ins = t.matmul(pS1[:, 384:512],
                           k_fm[:, d * 8192 + c * 128:d * 8192 + (c + 1) * 128],
                           q_fm[:, d * 8192 + c * 128:d * 8192 + (c + 1) * 128],
                           start=(d == 0), stop=(d == 1))
        p.inc(ins, "te")

    @block.vector
    def _(v):
        v.wait_ge(p.sem["te"], p.at("te"))
        v.tensor_scalar(z3[:], pS1[:, 0:128], tbl["s"][:, c:c + 1], None, ALU.subtract)
        v.tensor_tensor(lnbm[:], pS1[:, 128:256], msku[:], ALU.add)
        v.tensor_tensor(za[:], z3[:], lnbm[:], ALU.add)
        v.tensor_scalar(zb[:], z3[:], tbl["lnb"][:, c:c + 1], None, ALU.subtract)
        p.inc(v.tensor_tensor(zb[:], zb[:], mskl[:], ALU.add), "dv")
        v.wait_ge(p.sem["dv"], p.at("dv"))
        p.inc(v.tensor_tensor(z3[:], z3[:], mski[:], ALU.add), "dv")

    @block.scalar
    def _(s):
        s.wait_ge(p.sem["dv"], p.at("dv"))
        s.activation(za[:], za[:], AF.Exp)
        s.activation(zb[:], zb[:], AF.Exp, scale=-1.0)
        p.inc(s.activation(z3[:], z3[:], AF.Exp), "ac")

    @block.vector
    def _(v):
        v.wait_ge(p.sem["ac"], p.at("ac"))
        v.scalar_tensor_tensor(A1[:], za[:], -1.0, pS1[:, 256:384], ALU.mult, ALU.mult)
        v.scalar_tensor_tensor(A1T[:], zb[:], -1.0, pS1[:, 256:384], ALU.mult, ALU.mult)
        v.tensor_tensor(Aqk[:], z3[:], pS1[:, 384:512], ALU.mult)
        p.inc(v.tensor_tensor(Gt[:], A1[:], itile[:], ALU.add), "dv")

    @block.vector
    def _(v):
        v.tensor_copy(Xt_[:], A1[:])
        p.inc(v.tensor_copy(XTt[:], A1T[:]), "dv")

    for m in range(6):
        @block.tensor
        def _(t, m=m):
            t.wait_ge(p.sem["dv"], p.at("dv"))
            t.wait_ge(p.sem["ac"], p.at("ac"))
            t.matmul(pS2[:, 0:128], XTt[:], Xt_[:], start=True, stop=True)
            p.inc(t.matmul(pS2[:, 128:256], Xt_[:], XTt[:], start=True, stop=True), "te")

        @block.scalar
        def _(s, m=m):
            s.wait_ge(p.sem["te"], p.at("te"))
            s.copy(Xt_[:], pS2[:, 0:128])
            p.inc(s.copy(XTt[:], pS2[:, 128:256]), "ac")

        @block.tensor
        def _(t, m=m):
            t.wait_ge(p.sem["ac"], p.at("ac"))
            p.inc(t.matmul(pS2[:, 256:384], XTt[:], Gt[:], start=True, stop=True), "te")

        @block.vector
        def _(v, m=m):
            v.wait_ge(p.sem["te"], p.at("te"))
            p.inc(v.tensor_tensor(Gt[:], Gt[:], pS2[:, 256:384], ALU.add), "dv")

    @block.gpsimd
    def _(g):
        g.wait_ge(p.sem["dv"], p.at("dv"))
        g.wait_ge(p.sem["te"], p.at("te"))
        g.wait_ge(p.sem["lx"], p.kvlx[c])
        g.tensor_scalar(bV[:], vtc[c % 2][:], tbl["beta"][:, c:c + 1], None, ALU.mult)
        p.inc(g.tensor_scalar(bsK[:], ktc[c % 2][:], tbl["bEs"][:, c:c + 1],
                              None, ALU.mult), "gp")

    @block.scalar
    def _(s):
        s.wait_ge(p.sem["te"], p.at("te"))
        s.copy(Sb[0][:], S_f[0][:])
        p.inc(s.copy(Sb[1][:], S_f[1][:]), "ac")

    @block.tensor
    def _(t):
        t.wait_ge(p.sem["gp"], p.at("gp"))
        t.wait_ge(p.sem["ac"], p.at("ac"))
        t.matmul(pS2[:, 0:128], bsK[:, 0:128], Gt[:], start=True, stop=True)
        p.inc(t.matmul(pS2[:, 128:256], bsK[:, 128:256], Gt[:], start=True, stop=True), "te")

    @block.scalar
    def _(s):
        s.wait_ge(p.sem["te"], p.at("te"))
        s.mul(WtTn[:, 0:128], pS2[:, 0:128], -1.0)
        p.inc(s.mul(WtTn[:, 128:256], pS2[:, 128:256], -1.0), "ac")

    @block.tensor
    def _(t):
        t.wait_ge(p.sem["ac"], p.at("ac"))
        t.matmul(pU[:], Gt[:], bV[:], start=True, stop=False)
        t.matmul(pU[:], WtTn[:, 0:128], Sb[0][:], start=False, stop=False)
        p.inc(t.matmul(pU[:], WtTn[:, 128:256], Sb[1][:], start=False, stop=True), "te")

    @block.scalar
    def _(s):
        s.wait_ge(p.sem["te"], p.at("te"))
        p.inc(s.copy(U_bf[:], pU[:]), "ac")

    @block.gpsimd
    def _(g):
        g.wait_ge(p.sem["ac"], p.at("ac"))
        p.inc(g.tensor_scalar(Ubar[:], U_bf[:], tbl["EsCs"][:, c:c + 1], None, ALU.mult), "gp")

    @block.tensor
    def _(t):
        t.wait_ge(p.sem["gp"], p.at("gp"))
        t.wait_ge(p.sem["dv"], p.at("dv"))
        t.matmul(pSu[0][:], ktc[c % 2][:, 0:128], Ubar[:], start=True, stop=True)
        t.matmul(pSu[1][:], ktc[c % 2][:, 128:256], Ubar[:], start=True, stop=True)
        t.matmul(pOi[:], Aqk[:], U_bf[:], start=True, stop=True)
        t.matmul(pOe[:], q_fm[:, 0 * 8192 + c * 128:0 * 8192 + (c + 1) * 128], Sb[0][:],
                 start=True, stop=False)
        p.inc(t.matmul(pOe[:], q_fm[:, 1 * 8192 + c * 128:1 * 8192 + (c + 1) * 128], Sb[1][:],
                       start=False, stop=True), "te")

    @block.scalar
    def _(s):
        s.wait_ge(p.sem["te"], p.at("te"))
        p.inc(s.copy(o_in[:], pOi[:]), "ac")

    @block.vector
    def _(v):
        v.wait_ge(p.sem["te"], p.at("te"))
        v.wait_ge(p.sem["ac"], p.at("ac"))
        v.scalar_tensor_tensor(S_f[0][:], S_f[0][:], tbl["PC"][:, c:c + 1], pSu[0][:],
                               ALU.mult, ALU.add)
        v.scalar_tensor_tensor(S_f[1][:], S_f[1][:], tbl["PC"][:, c:c + 1], pSu[1][:],
                               ALU.mult, ALU.add)
        p.inc(v.scalar_tensor_tensor(o_raw[:], pOe[:], tbl["Es"][:, c:c + 1], o_in[:],
                                     ALU.mult, ALU.add), "dv")

    @block.scalar
    def _(s):
        s.wait_ge(p.sem["dv"], p.at("dv"))
        p.inc(s.activation(evac[:].bitcast(F32)[:, 0:256] if False else o1t[:],
                           o_raw[:], AF.Square, accum_out=W["tiny"][:, 1:2]), "ac")

    @block.vector
    def _(v):
        v.wait_ge(p.sem["ac"], p.at("ac"))
        p.inc(v.tensor_scalar(W["tiny"][:, 2:3], W["tiny"][:, 1:2], 1.0 / 512.0, 1e-5,
                              ALU.mult, ALU.add), "dv")
        v.wait_ge(p.sem["dv"], p.at("dv"))
        p.inc(v.reciprocal(W["tiny"][:, 3:4], W["tiny"][:, 2:3]), "dv")

    @block.scalar
    def _(s):
        s.wait_ge(p.sem["dv"], p.at("dv"))
        p.inc(s.activation(W["tiny"][:, 4:5], W["tiny"][:, 3:4], AF.Sqrt), "ac")

    @block.scalar
    def _(s, gld=gld):
        s.wait_ge(p.sem["ls"], gld)
        s.wait_ge(p.sem["gp"], p.at("gp"))
        p.inc(s.activation(sgt[:], gt_t[:], AF.Sigmoid), "ac")

    @block.vector
    def _(v):
        v.wait_ge(p.sem["ac"], p.at("ac"))
        p.inc(v.scalar_tensor_tensor(o1t[:], o_raw[:], W["tiny"][:, 4:5], normw_rep[:],
                                     ALU.mult, ALU.mult), "dv")

    @block.gpsimd
    def _(g):
        g.wait_ge(p.sem["dv"], p.at("dv"))
        g.wait_ge(p.sem["st"], p.at("st"))
        g.tensor_tensor(ggt[:], sgt[:], gt_t[:], ALU.mult)
        g.tensor_tensor(ggt[:], ggt[:], o1t[:], ALU.mult)
        p.inc(g.tensor_copy(og_t[:], ggt[:]), "gp")

    @block.sync
    def _(sync):
        sync.wait_ge(p.sem["gp"], p.at("gp"))
        sync.dma_start(out=og_o[c * 128:(c + 1) * 128, :], in_=og_t[:]).then_inc(p.sem["st"], 16)
        p.cnt["st"] += 16

    if c == 0 and W.get("dbg"):
        c0_o = W["c0_o"]

        @block.vector
        def _(v):
            v.wait_ge(p.sem["gp"], p.at("gp"))
            v.tensor_copy(c0_o[0:128, 0:128].bitcast(F32) if False else W["z3"][:], W["z3"][:])
            p.inc(v.engine_nop(), "dv")

        @block.sync
        def _(sync):
            sync.wait_ge(p.sem["dv"], p.at("dv"))
            for i, t_ in enumerate([W["A1"], W["Aqk"], W["Gt"]]):
                sync.dma_start(out=c0_o[:, i * 128:(i + 1) * 128].bitcast(BF16)[:, 0:128],
                               in_=t_[:]).then_inc(p.sem["st"], 16)
                p.cnt["st"] += 16
            sync.dma_start(out=c0_o[:, 512:1024], in_=W["o_raw"][:]).then_inc(p.sem["st"], 16)
            sync.dma_start(out=c0_o[:, 1024:1536], in_=W["o_in"][:]).then_inc(p.sem["st"], 16)
            sync.dma_start(out=c0_o[:, 1536:2048].bitcast(BF16)[:, 0:512],
                           in_=W["U_bf"][:]).then_inc(p.sem["st"], 16)
            sync.dma_start(out=c0_o[:, 2048:2560], in_=W["o1t"][:]).then_inc(p.sem["st"], 16)
            sync.dma_start(out=c0_o[:, 2560:3072].bitcast(BF16)[:, 0:512],
                           in_=W["bV"][:]).then_inc(p.sem["st"], 16)
            sync.dma_start(out=c0_o[:, 3072:3584].bitcast(BF16)[:, 0:512],
                           in_=W["og_t"][:]).then_inc(p.sem["st"], 16)
            p.cnt["st"] += 96


def build_launch_F():
    p = P()
    nc = p.nc
    o_fm = nc.dram_tensor("o_fm", [2048, 2048], BF16, kind="ExternalInput").ap()
    Wo1 = nc.dram_tensor("Wo1", [2048, 1024], F32, kind="ExternalInput").ap()
    xn1 = nc.dram_tensor("xn1", [2048, 1024], F32, kind="ExternalInput").ap()
    g1 = nc.dram_tensor("g1", [2048, 1024], F32, kind="ExternalInput").ap()
    x1 = nc.dram_tensor("x1", [2048, 1024], F32, kind="ExternalInput").ap()
    out_o = nc.dram_tensor("out", [2048, 1024], F32, kind="ExternalOutput").ap()

    obuf = p.sbuf("obufF", [128, 16 * 2048], BF16)
    wob = p.sbuf("wobF", [128, 16 * 1024], BF16)
    wtmp = p.sbuf("wtmpF", [128, 1024], F32)
    xnt = p.sbuf("xntF", [128, 1024], F32)
    g1t = p.sbuf("g1tF", [128, 1024], F32)
    x1t = p.sbuf("x1tF", [128, 1024], F32)
    s1t = p.sbuf("s1tF", [128, 1024], F32)
    ps = [p.psum(f"psF{i}", [128, 512], F32) for i in range(4)]

    for s in ["ls", "dv", "te", "st"]:
        p.semaphore(s)

    with p.ctx, nc.Block() as block:
        @block.sync
        def _(sync):
            for j in range(16):
                sync.dma_start(out=obuf[:, j * 2048:(j + 1) * 2048],
                               in_=o_fm[j * 128:(j + 1) * 128, :]).then_inc(p.sem["ls"], 16)
                p.cnt["ls"] += 16

        for j in range(16):
            @block.sync
            def _(sync, j=j):
                sync.wait_ge(p.sem["dv"], p.at("dv"))
                sync.dma_start(out=wtmp[:], in_=Wo1[j * 128:(j + 1) * 128, :]).then_inc(p.sem["ls"], 16)
                p.cnt["ls"] += 16
            ld_t = p.at("ls")

            @block.vector
            def _(v, j=j, ld_t=ld_t):
                v.wait_ge(p.sem["ls"], ld_t)
                p.inc(v.tensor_copy(wob[:, j * 1024:(j + 1) * 1024], wtmp[:]), "dv")

        wrdy_dv = p.at("dv")
        wrdy_ls = p.at("ls")

        for tc in range(16):
            @block.sync
            def _(sync, tc=tc):
                sync.wait_ge(p.sem["dv"], p.at("dv"))
                sync.dma_start(out=xnt[:], in_=xn1[tc * 128:(tc + 1) * 128, :]).then_inc(p.sem["ls"], 16)
                sync.dma_start(out=g1t[:], in_=g1[tc * 128:(tc + 1) * 128, :]).then_inc(p.sem["ls"], 16)
                sync.dma_start(out=x1t[:], in_=x1[tc * 128:(tc + 1) * 128, :]).then_inc(p.sem["ls"], 16)
                p.cnt["ls"] += 48
            rld = p.at("ls")

            for dh in range(2):
                @block.tensor
                def _(t, tc=tc, dh=dh, rld=rld):
                    t.wait_ge(p.sem["ls"], wrdy_ls)
                    t.wait_ge(p.sem["dv"], wrdy_dv)
                    t.wait_ge(p.sem["dv"], p.at("dv"))
                    bank = ps[(tc * 2 + dh) % 4]
                    ins = None
                    for j in range(16):
                        ins = t.matmul(bank[:],
                                       obuf[:, j * 2048 + tc * 128:j * 2048 + (tc + 1) * 128],
                                       wob[:, j * 1024 + dh * 512:j * 1024 + (dh + 1) * 512],
                                       start=(j == 0), stop=(j == 15))
                    p.inc(ins, "te")

            @block.vector
            def _(v, tc=tc, rld=rld):
                v.wait_ge(p.sem["te"], p.at("te"))
                v.wait_ge(p.sem["ls"], rld)
                v.wait_ge(p.sem["st"], p.at("st"))
                for dh in range(2):
                    bank = ps[(tc * 2 + dh) % 4]
                    v.tensor_tensor(s1t[:, dh * 512:(dh + 1) * 512], bank[:],
                                    xnt[:, dh * 512:(dh + 1) * 512], ALU.add)
                v.tensor_tensor(s1t[:], s1t[:], g1t[:], ALU.mult)
                p.inc(v.tensor_tensor(s1t[:], s1t[:], x1t[:], ALU.add), "dv")

            @block.sync
            def _(sync, tc=tc):
                sync.wait_ge(p.sem["dv"], p.at("dv"))
                sync.dma_start(out=out_o[tc * 128:(tc + 1) * 128, :], in_=s1t[:]).then_inc(p.sem["st"], 16)
                p.cnt["st"] += 16

        @block.sync
        def _(sync):
            sync.wait_ge(p.sem["st"], p.at("st"))

    return p.nc


from concourse.bass_utils import run_bass_kernel_spmd
import ml_dtypes as _mld

_BFn = _mld.bfloat16
_CORES = list(range(8))
_CC = np.ascontiguousarray


def kernel(**inputs):
    x = np.ascontiguousarray(np.asarray(inputs["L0_attn_q"], np.float32)).reshape(16384, 1024)
    wq = np.asarray(inputs["L0_attn_wq"], np.float32)
    wk = np.asarray(inputs["L0_attn_wk"], np.float32)
    wv = np.asarray(inputs["L0_attn_wv"], np.float32)
    wo = np.asarray(inputs["L0_attn_wo"], np.float32)
    wgate = np.asarray(inputs["L0_ffn_w_gate"], np.float32)
    wup = np.asarray(inputs["L0_ffn_w_up"], np.float32)
    wdn = np.asarray(inputs["L0_ffn_w_down"], np.float32)
    lnw = np.asarray(inputs["L1_attn_ln_w"], np.float32)[None, :]
    lnb = np.asarray(inputs["L1_attn_ln_b"], np.float32)[None, :]
    Wq1 = np.asarray(inputs["L1_attn_Wq"], np.float32)
    Wk1 = np.asarray(inputs["L1_attn_Wk"], np.float32)
    Wv1 = np.asarray(inputs["L1_attn_Wv"], np.float32)
    Wg1 = np.asarray(inputs["L1_attn_Wg"], np.float32)
    Wb1 = np.asarray(inputs["L1_attn_Wb"], np.float32)
    Wa1 = np.asarray(inputs["L1_attn_Wa"], np.float32)
    cq = np.asarray(inputs["L1_attn_conv_q"], np.float32)
    ck = np.asarray(inputs["L1_attn_conv_k"], np.float32)
    cv = np.asarray(inputs["L1_attn_conv_v"], np.float32)
    A_log = np.asarray(inputs["L1_attn_A_log"], np.float32)
    dtbv = np.asarray(inputs["L1_attn_dt_bias"], np.float32)
    nw = np.asarray(inputs["L1_attn_norm_w"], np.float32)
    Wo1 = np.asarray(inputs["L1_attn_Wo"], np.float32)
    Wfc = np.asarray(inputs["L1_ffn_W_fc"], np.float32)
    Wpr = np.asarray(inputs["L1_ffn_W_proj"], np.float32)

    resA = run_bass_kernel_spmd(build_launch_A(), [dict(
        x_own=_CC(x[c * 2048:(c + 1) * 2048]), wqT=_CC(wq.T), wkT=_CC(wk.T), wvT=_CC(wv.T),
        wg_sh=_CC(wgate[c * 352:(c + 1) * 352]), wu_sh=_CC(wup[c * 352:(c + 1) * 352]),
        wdn_sh=_CC(wdn[c * 128:(c + 1) * 128])) for c in _CORES], core_ids=_CORES).results
    q_fm = [r["q_fm"].view(_BFn) for r in resA]
    k_fm = [r["k_fm"].view(_BFn) for r in resA]
    v_tm = [r["v_tm"].view(_BFn) for r in resA]
    xq_tm = [r["xq_tm"].view(_BFn) for r in resA]
    wgq = np.concatenate([r["wgq_sh"].view(_BFn) for r in resA], 0)
    wuq = np.concatenate([r["wuq_sh"].view(_BFn) for r in resA], 0)
    wdnq = np.concatenate([r["wdnq_sh"].view(_BFn) for r in resA], 0)

    K_full = [np.concatenate(k_fm[4 * b:4 * b + 4], 1) for b in range(2)]
    V_full = [np.concatenate(v_tm[4 * b:4 * b + 4], 0) for b in range(2)]
    resB = run_bass_kernel_spmd(build_launch_B(), [dict(
        q_fm=_CC(q_fm[c]), k_fm=_CC(K_full[c // 4]), v_tm=_CC(V_full[c // 4]))
        for c in _CORES], core_ids=_CORES).results
    o_tm = [r["o_tm"].view(_BFn) for r in resB]

    i_idx = np.arange(2048)[:, None]
    j_idx = np.arange(1024)[None, :]
    inC = []
    for c in _CORES:
        b, rq = c // 4, c % 4
        out_full = np.concatenate(o_tm[4 * b:4 * b + 4], 0)
        gi = rq * 2048 + i_idx
        Y = out_full[(gi % 8) * 1024 + j_idx, gi // 8]
        inC.append(dict(y_fm=_CC(Y.T), woT=_CC(wo.T), x_own=_CC(x[c * 2048:(c + 1) * 2048]),
                        xq_fm=_CC(xq_tm[c].T), wgq_fm=_CC(wgq.T), wuq_fm=_CC(wuq.T),
                        wdnqT=_CC(wdnq.T)))
    resC = run_bass_kernel_spmd(build_launch_C(), inC, core_ids=_CORES).results
    x1 = [r["x1"] for r in resC]

    resD = run_bass_kernel_spmd(build_launch_D(), [dict(
        x1=_CC(x1[c]), lnw=_CC(lnw), lnb=_CC(lnb), WfcT=_CC(Wfc.T), WprojT=_CC(Wpr.T))
        for c in _CORES], core_ids=_CORES).results
    xn1 = [r["xn1"] for r in resD]
    nrm_fm = [r["normed_fm"].view(_BFn) for r in resD]
    g1 = [r["g1"] for r in resD]

    NRM = [np.concatenate(nrm_fm[4 * b:4 * b + 4], 1) for b in range(2)]
    inE = []
    for c in _CORES:
        b, h = c // 4, c % 4
        inE.append(dict(
            nrm_fm=_CC(NRM[b]), wqh=_CC(Wq1[:, h * 256:(h + 1) * 256]),
            wkh=_CC(Wk1[:, h * 256:(h + 1) * 256]), wvh=_CC(Wv1[:, h * 512:(h + 1) * 512]),
            wgh=_CC(Wg1[:, h * 512:(h + 1) * 512]),
            wbh=_CC(np.stack([Wb1[:, h], Wa1[:, h]], 1)),
            convq=_CC(cq[h * 256:(h + 1) * 256]), convk=_CC(ck[h * 256:(h + 1) * 256]),
            convv=_CC(cv[h * 512:(h + 1) * 512]),
            scal=_CC(np.array([[A_log[h], dtbv[h]]], np.float32)), normw=_CC(nw[None, :])))
    resE = run_bass_kernel_spmd(build_launch_E(), inE, core_ids=_CORES).results
    og = [r["og_tm"].view(_BFn) for r in resE]

    inF = []
    for c in _CORES:
        b, rq = c // 4, c % 4
        o_own = np.concatenate([og[4 * b + h][rq * 2048:(rq + 1) * 2048, :] for h in range(4)], 1)
        inF.append(dict(o_fm=_CC(o_own.T), Wo1=_CC(Wo1), xn1=_CC(xn1[c]), g1=_CC(g1[c]),
                        x1=_CC(x1[c])))
    resF = run_bass_kernel_spmd(build_launch_F(), inF, core_ids=_CORES).results
    out = np.concatenate([r["out"] for r in resF], 0).reshape(2, 8192, 1024)
    return out
